# version 13
# baseline (speedup 1.0000x reference)
import math
import contextlib
import numpy as np
import concourse.bass as bass
import concourse.mybir as mybir
from concourse.bass_utils import run_bass_kernel_spmd

F32 = mybir.dt.float32
BF16 = mybir.dt.bfloat16
I16 = mybir.dt.int16
AF = mybir.ActivationFunctionType
ALU = mybir.AluOpType

HID = 2048
STATE = 128
HEADS = 64
HEADDIM = 64
KCONV = 4
INTER = 4096
CONVD = INTER + 2 * STATE
PROJ = INTER + CONVD + HEADS
FFH = 8192
EPS = 1e-5
GELU_A = float(2.0 * math.sqrt(2.0 / math.pi))
GELU_B = float(2.0 * math.sqrt(2.0 / math.pi) * 0.044715)
NCORES = 8
Q = 128
NKT = HID // 128
N_XS_T = INTER // 128
N_XBC_T = CONVD // 128


class Prog:

    ENGS = ("v", "s", "t", "g", "y")

    def __init__(self, nc, ctx):
        self.nc = nc
        self.ctx = ctx
        E = ctx.enter_context
        self.sem = {k: E(nc.semaphore(f"pg_{k}")) for k in self.ENGS}
        self.cnt = {k: 0 for k in self.ENGS}
        self.dma_slots = {}
        self.cc_sem = E(nc.semaphore("cc_sem"))
        self.cc_cnt = 0
        self.last_w = {}
        self.readers = {}
        self.queues = {k: [] for k in self.ENGS}
        self.waited = {k: {} for k in self.ENGS}

    def _slot(self, name):
        if name not in self.dma_slots:
            self.dma_slots[name] = [self.ctx.enter_context(self.nc.semaphore(f"dq_{name}")), 0]
        return self.dma_slots[name]

    def _tok_sem(self, tok):
        kind = tok[0]
        if kind == "e":
            return ("pg_" + tok[1], self.sem[tok[1]], tok[2])
        if kind == "d":
            return ("dq_" + tok[1], self.dma_slots[tok[1]][0], 16 * self.dma_slots[tok[1]][1])
        return ("cc", self.cc_sem, tok[1])

    @staticmethod
    def _ex(regs):
        out = []
        for g in regs:
            if g == "psB":
                out.extend(("psB0", "psB1"))
            else:
                out.append(g)
        return out

    def _collect(self, ek, r, w, slots_out=None):
        r = self._ex(r)
        w = self._ex(w)
        waits = []
        seen = self.waited[ek]
        toks = []
        for reg in r:
            t = self.last_w.get(reg)
            if t is not None:
                toks.append(t)
            if reg.startswith("ps"):
                toks.extend(t2 for t2 in self.readers.get(reg, ())
                            if not (t2[0] == "e" and t2[1] == ek))
        for reg in w:
            t = self.last_w.get(reg)
            if t is not None:
                toks.append(t)
            toks.extend(self.readers.get(reg, ()))
        for t in toks:
            if t[0] == "d" and slots_out is not None:
                slots_out.add(t[1])
            key, sem, val = self._tok_sem(t)
            if seen.get(key, 0) < val:
                seen[key] = val
                waits.append((sem, val))
        return waits

    def _record(self, tok, r, w, slots=()):
        r = self._ex(r)
        w = self._ex(w)
        for reg in w:
            if reg.startswith("@"):
                self.readers[reg] = []
                continue
            self.last_w[reg] = tok
            self.readers[reg] = []
        for reg in r:
            self.readers.setdefault(reg, []).append(tok)
        for sname in slots:
            self.readers.setdefault("@" + sname, []).append(tok)

    def op(self, ek, f, r=(), w=()):
        sl = set()
        waits = self._collect(ek, r, w, sl)
        self.cnt[ek] += 1
        sem = self.sem[ek]

        def emit(e, waits=waits, f=f, sem=sem):
            for s, v in waits:
                e.wait_ge(s, v)
            f(e).then_inc(sem, 1)
        self.queues[ek].append(emit)
        self._record(("e", ek, self.cnt[ek]), r, w, sl)

    def ops(self, ek, fs, r=(), w=()):
        sl = set()
        waits = self._collect(ek, r, w, sl)
        self.cnt[ek] += 1
        sem = self.sem[ek]

        def emit(e, waits=waits, fs=tuple(fs), sem=sem):
            for s, v in waits:
                e.wait_ge(s, v)
            ins = None
            for f in fs:
                ins = f(e)
            ins.then_inc(sem, 1)
        self.queues[ek].append(emit)
        self._record(("e", ek, self.cnt[ek]), r, w, sl)

    def dma(self, ek, sname, out, in_, r=(), w=(), slow=False):
        slot = self._slot(sname)
        sl = set()
        waits = self._collect(ek, list(r), list(w) + ["@" + sname], sl)
        slot[1] += 1
        sem = slot[0]

        def emit(e, waits=waits, out=out, in_=in_, sem=sem, slow=slow):
            for s, v in waits:
                e.wait_ge(s, v)
            e.dma_start(out=out, in_=in_, allow_slow_non_contiguous=slow).then_inc(sem, 16)
        self.queues[ek].append(emit)
        self._record(("d", sname, slot[1]), list(r), list(w) + ["@" + sname], sl)

    def collective(self, ins_ap, outs_ap, r=(), w=()):
        sl = set()
        waits = self._collect("g", r, w, sl)
        self.cc_cnt += 1
        ccs = self.cc_sem

        def emit(e, waits=waits, ins_ap=ins_ap, outs_ap=outs_ap, ccs=ccs):
            for s, v in waits:
                e.wait_ge(s, v)
            e.collective_compute(
                "AllGather", ALU.bypass,
                replica_groups=[list(range(NCORES))],
                ins=[ins_ap], outs=[outs_ap], cc_dim="Partition",
            ).then_inc(ccs, 1)
        self.queues["g"].append(emit)
        self._record(("c", self.cc_cnt), r, w, sl)

    def join(self, skip_cc=True):
        toks = [("e", k, self.cnt[k]) for k in self.ENGS if self.cnt[k] > 0]
        toks += [("d", n, s[1]) for n, s in self.dma_slots.items() if s[1] > 0]
        if self.cc_cnt and not skip_cc:
            toks.append(("c", self.cc_cnt))
        for ek in self.ENGS:
            waits = []
            seen = self.waited[ek]
            for t in toks:
                if t[0] == "e" and t[1] == ek:
                    continue
                key, sem, val = self._tok_sem(t)
                if seen.get(key, 0) < val:
                    seen[key] = val
                    waits.append((sem, val))
            if waits:
                def emit(e, waits=waits):
                    for s, v in waits:
                        e.wait_ge(s, v)
                self.queues[ek].append(emit)

    def wait_region(self, ek, *regs):
        waits = self._collect(ek, regs, ())
        if waits:
            def emit(e, waits=waits):
                for s, v in waits:
                    e.wait_ge(s, v)
            self.queues[ek].append(emit)

    def emit_all(self):
        nc = self.nc
        with nc.Block() as block:
            @block.vector
            def _(e):
                for f in self.queues["v"]:
                    f(e)

            @block.scalar
            def _(e):
                for f in self.queues["s"]:
                    f(e)

            @block.tensor
            def _(e):
                for f in self.queues["t"]:
                    f(e)

            @block.gpsimd
            def _(e):
                for f in self.queues["g"]:
                    f(e)

            @block.sync
            def _(e):
                for f in self.queues["y"]:
                    f(e)


def ts(i, n=128):
    return slice(i * n, (i + 1) * n)



def build(T=2048, dbg=False, stop_after=None):
    NCH = T // Q
    TH = T + 3
    NTT_R = T // 128
    nc = bass.Bass("TRN2", target_bir_lowering=False, num_devices=NCORES)

    di = {}
    def inp(name, shape):
        di[name] = nc.dram_tensor(name, shape, F32, kind="ExternalInput")
        return di[name]

    x_sh = inp("x_sh", [TH, HID])
    WpT = inp("WpT", [HID, PROJ])
    WoT = inp("WoT", [INTER, HID])
    w1T_sl = inp("w1T_sl", [HID, FFH // NCORES])
    w2T_sl = inp("w2T_sl", [FFH, HID // NCORES])
    convdiag = inp("convdiag", [N_XBC_T * KCONV * 128, 128])
    conv_b = inp("conv_b", [CONVD])
    nw_in = inp("nw_in", [HID])
    gnw_in = inp("gnw_in", [INTER])
    dtb_in = inp("dtb_in", [HEADS])
    alog_in = inp("alog_in", [HEADS])
    dd_in = inp("dd_in", [HEADS])
    ident_in = inp("ident_in", [128, 128])
    tri01_in = inp("tri01_in", [128, 128])
    pexp_in = inp("pexp_in", [HEADS, INTER])
    m2l_in = inp("m2l_in", [9, NCORES])

    out_y = nc.dram_tensor("out_y", [T, HID], F32, kind="ExternalOutput")

    d_xs = nc.dram_tensor("d_xs", [INTER, T], BF16)
    d_gate = nc.dram_tensor("d_gate", [T, INTER], BF16)
    d_ypart = nc.dram_tensor("d_ypart", [T, INTER], BF16)
    d_sloc = nc.dram_tensor("d_sloc", [NCH * 128, INTER], BF16)
    d_srow = nc.dram_tensor("d_srow", [NCH, HEADS * Q], F32)
    d_ynt = nc.dram_tensor("d_ynt", [INTER, T], BF16)
    d_x2 = nc.dram_tensor("d_x2", [T, HID], F32)
    d_x2i = nc.dram_tensor("d_x2i", [HID, T], BF16)
    d_hg = nc.dram_tensor("d_hg", [T, FFH], F32)
    d_hint = nc.dram_tensor("d_hint", [FFH, T], BF16)
    d_tmp = nc.dram_tensor("d_tmp", [1, FFH], F32)

    W1SL = FFH // NCORES
    W2SL = HID // NCORES
    AGW1 = HID * W1SL
    AGW2 = FFH * W2SL
    ag_w1_in = nc.dram_tensor("ag_w1_in", [HID, W1SL], BF16)
    ag_w1_out = nc.dram_tensor("ag_w1_out", [NCORES * HID, W1SL], BF16)
    ag_w2_in = nc.dram_tensor("ag_w2_in", [FFH, W2SL], BF16)
    ag_w2_out = nc.dram_tensor("ag_w2_out", [NCORES * FFH, W2SL], BF16)
    ag_sc_in = nc.dram_tensor("ag_sc_in", [1, W1SL + W2SL], F32)
    ag_sc_out = nc.dram_tensor("ag_sc_out", [NCORES, W1SL + W2SL], F32)
    STN = 128 * INTER + HEADS
    ag_st_in = nc.dram_tensor("ag_st_in", [1, STN], F32)
    ag_st_out = nc.dram_tensor("ag_st_out", [NCORES, STN], F32)

    dbg_t = {}
    if dbg:
        for name, shape in [
            ("dbg_xnt", [128, NKT * TH]), ("dbg_xs", [INTER, T]),
            ("dbg_dt", [HEADS, T]), ("dbg_gate", [T, INTER]),
            ("dbg_ypart", [T, INTER]), ("dbg_srun", [128, INTER]),
            ("dbg_enter", [128, INTER]), ("dbg_y", [T, INTER]),
            ("dbg_x2", [T, HID]), ("dbg_hg", [T, FFH]),
            ("dbg_w1q", [NCORES * HID, W1SL]), ("dbg_sc", [NCORES, W1SL + W2SL]),
            ("dbg_hint", [FFH, T]), ("dbg_w2q", [NCORES * FFH, HID // NCORES]),
            ("dbg_ff", [T, HID]),
            ("dbg_s", [HEADS, T]), ("dbg_M", [128, 128]),
        ]:
            dbg_t[name] = nc.dram_tensor(name, shape, F32, kind="ExternalOutput")

    ctx = contextlib.ExitStack()
    E = ctx.enter_context
    p = Prog(nc, ctx)
    open_scopes = []

    def colsplits(total, step, base=0):
        out = []
        c = base
        while c < total:
            w = min(step, total - c)
            out.append((c, w))
            c += w
        return out

    ident_f = E(nc.sbuf_tensor("ident_f", [128, 128], F32))
    ident_b = E(nc.sbuf_tensor("ident_b", [128, 128], BF16))
    tri01 = E(nc.sbuf_tensor("tri01", [128, 128], F32))
    onescol_b = E(nc.sbuf_tensor("onescol_b", [128, 1], BF16))
    ones1f = E(nc.sbuf_tensor("ones1f", [1, 128], F32))
    cst = E(nc.sbuf_tensor("cst", [128, 8], F32))
    nw_sb = E(nc.sbuf_tensor("nw_sb", [128, NKT], F32))
    cb_sb = E(nc.sbuf_tensor("cb_sb", [128, N_XBC_T], F32))
    dtb_sb = E(nc.sbuf_tensor("dtb_sb", [HEADS, 1], F32))
    negA = E(nc.sbuf_tensor("negA", [HEADS, 1], F32))
    m2l_sb = E(nc.sbuf_tensor("m2l_sb", [9, NCORES], F32))
    tmp64 = E(nc.sbuf_tensor("tmp64", [HEADS, 1], F32))
    ld_sb = E(nc.sbuf_tensor("ld_sb", [HEADS, 1], F32))
    x2scl = E(nc.sbuf_tensor("x2scl", [128, NTT_R], F32))
    x2inv = E(nc.sbuf_tensor("x2inv", [128, NTT_R], F32))
    hscl = E(nc.sbuf_tensor("hscl", [128, NTT_R], F32))
    hinv = E(nc.sbuf_tensor("hinv", [128, NTT_R], F32))
    hmax_run = E(nc.sbuf_tensor("hmax_run", [128, NTT_R], F32))
    ws2col = E(nc.sbuf_tensor("ws2col", [128, HID // 128], F32))

    psA = E(nc.psum_tensor("psA", [128, 2048], F32))
    psB = E(nc.psum_tensor("psB", [128, 1024], F32))
    psT = E(nc.psum_tensor("psT", [128, 1024], BF16))
    psT2 = E(nc.psum_tensor("psT2", [128, 1024], BF16))
    mstack = contextlib.ExitStack()
    open_scopes.append(mstack)
    EM = mstack.enter_context
    gnw_rep = EM(nc.sbuf_tensor("gnw_rep", [128, INTER], BF16))
    drep = EM(nc.sbuf_tensor("drep", [128, HEADS], F32))
    dt_sb = EM(nc.sbuf_tensor("dt_sb", [HEADS, T], F32))
    dtA_sb = EM(nc.sbuf_tensor("dtA_sb", [HEADS, T], F32))
    s_sb = EM(nc.sbuf_tensor("s_sb", [HEADS, T], F32))
    smld_sb = EM(nc.sbuf_tensor("smld_sb", [HEADS, T], F32))
    smldT = EM(nc.sbuf_tensor("smldT", [128, HEADS * NCH], F32))
    sexpT = EM(nc.sbuf_tensor("sexpT", [128, HEADS * NCH], F32))
    dexp = EM(nc.sbuf_tensor("dexp", [128, 32 * NCH], F32))
    B_sb = EM(nc.sbuf_tensor("B_sb", [128, T], BF16))
    C_sb = EM(nc.sbuf_tensor("C_sb", [128, T], BF16))
    onesrow64 = EM(nc.sbuf_tensor("onesrow64", [HEADS, Q], F32))
    PSA01 = ["psA0", "psA1"]


    class _StopBuild(Exception):
        pass

    def _stop(tag):
        if stop_after == tag:
            p.join(skip_cc=False)
            p.emit_all()
            raise _StopBuild()
        return False
    try:
        p.dma("y", "cst", ident_f[:, :], ident_in[:, :], w=["ident_f"])
        p.dma("g", "cstg", ident_b[:, :], ident_in[:, :], w=["ident_b"])
        p.dma("y", "cst", tri01[:, :], tri01_in[:, :], w=["tri01"])
        p.dma("y", "cst", m2l_sb[:, :], m2l_in[:, :], w=["m2l"])
        p.dma("y", "cst", cb_sb[:, :], bass.AP(conv_b, 0, [[1, 128], [128, N_XBC_T]]), w=["cb"], slow=True)
        p.dma("y", "cst", nw_sb[:, :], bass.AP(nw_in, 0, [[1, 128], [128, NKT]]), w=["nw"], slow=True)
        p.dma("y", "cst", dtb_sb[:, :], bass.AP(dtb_in, 0, [[1, HEADS], [1, 1]]), w=["dtb"])
        p.dma("y", "cst", tmp64[:, :], bass.AP(alog_in, 0, [[1, HEADS], [1, 1]]), w=["tmp64"])
        p.op("v", lambda e: e.memset(onescol_b[:, :], 1.0), w=["ones"])
        p.op("v", lambda e: e.memset(ones1f[:, :], 1.0), w=["ones1f"])
        p.op("v", lambda e: e.memset(onesrow64[:, :], 1.0), w=["ones64"])
        p.op("v", lambda e: e.memset(cst[:, 0:1], EPS), w=["cst"])
        p.op("v", lambda e: e.memset(cst[:, 1:2], 87.0), w=["cst87"])
        p.op("v", lambda e: e.memset(cst[:, 2:3], 1.0), w=["cst1"])
        p.op("s", lambda e: e.activation(negA[:, :], tmp64[:, :], AF.Exp), r=["tmp64"], w=["negA"])
        p.op("v", lambda e: e.tensor_scalar_mul(negA[:, :], negA[:, :], -1.0), r=["negA"], w=["negA"])

        def bcast_row(row_ap, width, dest_sb, dest_reg, rregs, scale=None, dt_=None):
            fs = []
            for (c0, w) in colsplits(width, 512):
                fs.append(lambda e, c0=c0, w=w: e.matmul(psA[:, c0 % 2048:c0 % 2048 + w],
                                                         ones1f[:, :], row_ap[0:1, c0:c0 + w],
                                                         start=True, stop=True))
            p.ops("t", fs, r=["ones1f"] + list(rregs), w=PSA01)
            for (c0, w) in colsplits(width, 2048):
                wv = min(2048, width - c0)
                if scale is None:
                    p.op("v", lambda e, c0=c0, wv=wv: e.tensor_copy(dest_sb[:, c0:c0 + wv], psA[:, 0:wv]),
                         r=PSA01, w=[dest_reg])
                else:
                    p.op("v", lambda e, c0=c0, wv=wv: e.tensor_scalar_mul(dest_sb[:, c0:c0 + wv], psA[:, 0:wv], scale),
                         r=PSA01, w=[dest_reg])
            assert width <= 2048

        with contextlib.ExitStack() as p0:
            grow = p0.enter_context(nc.sbuf_tensor("grow", [1, INTER], F32))
            p.dma("y", "cst", grow[0:1, 0:HEADS], bass.AP(dd_in, 0, [[HEADS, 1], [1, HEADS]]), w=["grow"])
            bcast_row(grow, HEADS, drep, "drep", ["grow"])
            p.dma("y", "cst", grow[0:1, 0:INTER], bass.AP(gnw_in, 0, [[INTER, 1], [1, INTER]]),
                  r=["grow"], w=["grow"])
            for (c0, w) in colsplits(INTER, 2048):
                fs = []
                for (s0, sw) in colsplits(w, 512):
                    fs.append(lambda e, c0=c0, s0=s0, sw=sw: e.matmul(psA[:, s0:s0 + sw], ones1f[:, :],
                                                                     grow[0:1, c0 + s0:c0 + s0 + sw],
                                                                     start=True, stop=True))
                p.ops("t", fs, r=["ones1f", "grow"], w=PSA01)
                p.op("v", lambda e, c0=c0, w=w: e.tensor_copy(gnw_rep[:, c0:c0 + w], psA[:, 0:w]),
                     r=PSA01, w=["gnw_rep"])

        p.join()
        with contextlib.ExitStack() as p1:
            E1 = p1.enter_context
            wqf = E1(nc.sbuf_tensor("wqf", [128, NKT * 1024], F32))
            absb = E1(nc.sbuf_tensor("absb", [128, 1024], BF16))
            s1row = E1(nc.sbuf_tensor("s1row", [1, W1SL], F32))
            s2row = E1(nc.sbuf_tensor("s2row", [1, W2SL], F32))
            shrep = E1(nc.sbuf_tensor("shrep", [128, 1024], F32))
            nshrep = E1(nc.sbuf_tensor("nshrep", [128, 1024], F32))
            c1b = E1(nc.sbuf_tensor("c1b", [128, 1024], BF16))
            c2b = E1(nc.sbuf_tensor("c2b", [128, 1024], BF16))
            qb = E1(nc.sbuf_tensor("qb", [128, 1024], BF16))

            def quant_weight(wsl_dram, nkt_w, WSL, agin, srow, tag):
                for k in range(nkt_w):
                    p.dma("y", "w1s", wqf[:, k * WSL:(k + 1) * WSL], wsl_dram[ts(k), :], w=[f"{tag}wqf{k}"])
                for k in range(nkt_w):
                    p.op("s", lambda e, k=k, WSL=WSL: e.activation(absb[:, 0:WSL], wqf[:, k * WSL:(k + 1) * WSL], AF.Abs),
                         r=[f"{tag}wqf{k}"], w=["absb"])
                    fs = []
                    for (c0, w) in colsplits(WSL, 512):
                        fs.append(lambda e, c0=c0, w=w, k=k: e.matmul(psB[0:1, c0:c0 + w], onescol_b[:, :],
                                                                      absb[:, c0:c0 + w],
                                                                      start=(k == 0), stop=(k == nkt_w - 1)))
                    p.ops("t", fs, r=["absb", "ones"], w=["psB"])
                p.op("v", lambda e, WSL=WSL, nkt_w=nkt_w, srow=srow: e.tensor_scalar(
                    srow[0:1, 0:WSL], psB[0:1, 0:WSL], 1.0 / (nkt_w * 128.0), 1e-5, ALU.mult, ALU.max),
                    r=["psB"], w=[f"{tag}srow"])
                fs = []
                for (c0, w) in colsplits(WSL, 512):
                    fs.append(lambda e, c0=c0, w=w, srow=srow: e.matmul(psA[:, c0:c0 + w], ones1f[:, :],
                                                                       srow[0:1, c0:c0 + w], start=True, stop=True))
                p.ops("t", fs, r=["ones1f", f"{tag}srow"], w=PSA01)
                p.op("v", lambda e, WSL=WSL: e.tensor_scalar_mul(shrep[:, 0:WSL], psA[:, 0:WSL], 0.5),
                     r=PSA01, w=["shrep"])
                p.op("v", lambda e, WSL=WSL: e.tensor_scalar_mul(nshrep[:, 0:WSL], psA[:, 0:WSL], -0.5),
                     r=PSA01, w=["nshrep"])
                for k in range(nkt_w):
                    wt = wqf[:, k * WSL:(k + 1) * WSL]
                    p.op("v", lambda e, wt=wt, WSL=WSL: e.tensor_tensor(c1b[:, 0:WSL], wt, shrep[:, 0:WSL], ALU.is_gt),
                         r=[f"{tag}wqf{k}", "shrep"], w=["c1b"])
                    p.op("v", lambda e, wt=wt, WSL=WSL: e.tensor_tensor(c2b[:, 0:WSL], wt, nshrep[:, 0:WSL], ALU.is_lt),
                         r=[f"{tag}wqf{k}", "nshrep"], w=["c2b"])
                    p.op("v", lambda e, WSL=WSL: e.tensor_tensor(qb[:, 0:WSL], c1b[:, 0:WSL], c2b[:, 0:WSL], ALU.subtract),
                         r=["c1b", "c2b"], w=["qb"])
                    p.dma("s", "wqo", agin[ts(k), :], qb[:, 0:WSL], r=["qb"], w=[f"{tag}ag{k}"])

            quant_weight(w1T_sl, NKT, W1SL, ag_w1_in, s1row, "w1")
            quant_weight(w2T_sl, FFH // 128, W2SL, ag_w2_in, s2row, "w2")
            p.dma("s", "wqo", ag_sc_in[0:1, 0:W1SL], s1row[0:1, :], r=["w1srow"], w=["agsc1"])
            p.dma("s", "wqo", ag_sc_in[0:1, W1SL:W1SL + W2SL], s2row[0:1, :], r=["w2srow"], w=["agsc2"])
            p.collective(bass.AP(ag_w1_in, 0, [[1, 1], [1, AGW1]]), bass.AP(ag_w1_out, 0, [[1, 1], [1, NCORES * AGW1]]),
                         r=[f"w1ag{k}" for k in range(NKT)], w=["AGW1"])
            p.collective(bass.AP(ag_w2_in, 0, [[1, 1], [1, AGW2]]), bass.AP(ag_w2_out, 0, [[1, 1], [1, NCORES * AGW2]]),
                         r=[f"w2ag{k}" for k in range(FFH // 128)], w=["AGW2"])
            p.collective(bass.AP(ag_sc_in, 0, [[1, 1], [1, W1SL + W2SL]]),
                         bass.AP(ag_sc_out, 0, [[1, 1], [1, NCORES * (W1SL + W2SL)]]),
                         r=["agsc1", "agsc2"], w=["AGSC"])

        _stop('P1')
        p.join()
        with contextlib.ExitStack() as P2:
            E2o = P2.enter_context
            x_nT = E2o(nc.sbuf_tensor("x_nT", [128, NKT * TH], BF16))
            nwrep = E2o(nc.sbuf_tensor("nwrep", [128, HID], F32))
            growp = E2o(nc.sbuf_tensor("growp", [1, HID], F32))
            p.dma("y", "cst", growp[0:1, 0:HID], bass.AP(nw_in, 0, [[HID, 1], [1, HID]]), w=["growp"])
            bcast_row(growp, HID, nwrep, "nwrep", ["growp"])
            NTT = (TH + 127) // 128
            with contextlib.ExitStack() as p2a:
                E2 = p2a.enter_context
                xld = [E2(nc.sbuf_tensor(f"xld{i}", [128, HID], F32)) for i in range(2)]
                xsq = E2(nc.sbuf_tensor("xsq", [128, HID], F32))
                sscol = E2(nc.sbuf_tensor("sscol", [128, 1], F32))
                rstd = E2(nc.sbuf_tensor("rstd", [128, 1], F32))
                xnb = E2(nc.sbuf_tensor("xnb", [128, HID], BF16))
                for r in range(NTT):
                    nrow = min(128, TH - r * 128)
                    bi = r % 2
                    p.dma("y", f"xl{bi}", xld[bi][0:nrow, :], x_sh[r * 128:r * 128 + nrow, :], w=[f"xld{bi}"])
                    p.op("s", lambda e, b=bi, n=nrow: e.activation(xsq[0:n, :], xld[b][0:n, :], AF.Square,
                                                                   accum_out=sscol[0:n, :]),
                         r=[f"xld{bi}"], w=["xsq", "sscol"])
                    p.op("s", lambda e, n=nrow: e.activation(rstd[0:n, :], sscol[0:n, :], AF.Ln,
                                                             bias=cst[0:n, 0:1], scale=1.0 / HID),
                         r=["sscol", "cst"], w=["rstd"])
                    p.op("s", lambda e, n=nrow: e.activation(rstd[0:n, :], rstd[0:n, :], AF.Exp, scale=-0.5),
                         r=["rstd"], w=["rstd"])
                    p.op("v", lambda e, b=bi, n=nrow: e.scalar_tensor_tensor(xnb[0:n, :], xld[b][0:n, :], rstd[0:n, 0:1],
                                                                             nwrep[0:n, :], ALU.mult, ALU.mult),
                         r=[f"xld{bi}", "rstd", "nwrep"], w=["xnb"])
                    for half in range(2):
                        p.ops("t", [(lambda e, j=j, h=half, n=nrow:
                                     e.transpose(psT[:, j * 128:j * 128 + n], xnb[0:n, ts(h * 8 + j)], ident_b[0:n, 0:n]))
                                    for j in range(8)],
                              r=["xnb", "ident_b"], w=["psT"])
                        p.op("v", lambda e, h=half, r0=r, n=nrow: e.tensor_copy(
                                 bass.AP(x_nT, (h * 8) * TH + r0 * 128, [[NKT * TH, 128], [TH, 8], [1, n]]),
                                 bass.AP(psT, 0, [[1024, 128], [128, 8], [1, n]])),
                             r=["psT"], w=[f"xnT_{r}_{half}"])
            XNT_ALL = [f"xnT_{r}_{h}" for r in range(NTT) for h in range(2)]
            if dbg:
                p.join()
                for k in range(NKT):
                    p.dma("g", "dbgo2", dbg_t["dbg_xnt"][:, k * TH:(k + 1) * TH],
                          x_nT[:, k * TH:(k + 1) * TH], r=XNT_ALL, w=["dbg1"])

            p.join()
            PSA_W = min(TH, 2048)
            with contextlib.ExitStack() as p2b:
                E2 = p2b.enter_context
                wp_b = [E2(nc.sbuf_tensor(f"wp_b{i}", [128, NKT * 512], BF16)) for i in range(2)]
                cdg_b = [E2(nc.sbuf_tensor(f"cdg_b{i}", [128, 16 * 128], BF16)) for i in range(2)]
                xbc_sb = E2(nc.sbuf_tensor("xbc_sb", [128, TH], BF16))
                xs_out = E2(nc.sbuf_tensor("xs_out", [128, T], BF16))

                def load_wchunk(ci, c0, w):
                    p.dma("g", f"wpc{ci % 2}", bass.AP(wp_b[ci % 2], 0, [[NKT * 512, 128], [512, NKT], [1, w]]),
                          bass.AP(WpT, c0, [[PROJ, 128], [128 * PROJ, NKT], [1, w]]),
                          w=[f"wpb{ci % 2}"])

                def load_cdg(ci, ctl0, ntile):
                    p.dma("g", f"cdg{ci % 2}", bass.AP(cdg_b[ci % 2], 0, [[16 * 128, 128], [128, ntile * KCONV], [1, 128]]),
                          bass.AP(convdiag, ctl0 * KCONV * 128 * 128, [[128, 128], [128 * 128, ntile * KCONV], [1, 128]]),
                          w=[f"cdgb{ci % 2}"])

                chunks = [(INTER + 512 * i, 512) for i in range(8)] + [(INTER + 4096, 320)]
                load_wchunk(0, chunks[0][0], chunks[0][1])
                load_cdg(0, 0, 4)
                for ci, (c0, wdt) in enumerate(chunks):
                    if ci + 1 < len(chunks):
                        load_wchunk(ci + 1, chunks[ci + 1][0], chunks[ci + 1][1])
                        load_cdg(ci + 1, (ci + 1) * 4, 4 if ci + 1 < 8 else 2)
                    wreg = f"wpb{ci % 2}"
                    creg = f"cdgb{ci % 2}"
                    for tl in range((wdt + 127) // 128):
                        is_dt = (ci == 8 and tl == 2)
                        m = 64 if is_dt else 128
                        wb = wp_b[ci % 2]
                        fs = []
                        for k in range(NKT):
                            for (n0, nw) in colsplits(PSA_W, 512):
                                fs.append(lambda e, k=k, n0=n0, nw=nw, m=m, tl=tl, wb=wb: e.matmul(
                                    psA[0:m, n0:n0 + nw], wb[:, k * 512 + tl * 128: k * 512 + tl * 128 + m],
                                    x_nT[:, k * TH + n0: k * TH + n0 + nw],
                                    start=(k == 0), stop=(k == NKT - 1)))
                            if TH > 2048:
                                fs.append(lambda e, k=k, m=m, tl=tl, wb=wb: e.matmul(
                                    psB[0:m, 0:TH - 2048],
                                    wb[:, k * 512 + tl * 128: k * 512 + tl * 128 + m],
                                    x_nT[:, k * TH + 2048: k * TH + TH],
                                    start=(k == 0), stop=(k == NKT - 1)))
                        p.ops("t", fs, r=[wreg] + XNT_ALL, w=PSA01 + ["psB"])
                        ctl = ci * 4 + tl
                        if is_dt:
                            na = min(T, 2048 - 3)
                            p.op("s", lambda e, na=na: e.activation(dt_sb[:, 0:na], psA[0:HEADS, 3:3 + na], AF.Identity,
                                                                    bias=dtb_sb[:, :]),
                                 r=PSA01 + ["dtb"], w=["dt_a"])
                            if TH > 2048:
                                p.op("s", lambda e, na=na: e.activation(dt_sb[:, na:T], psB[0:HEADS, 0:T - na],
                                                                        AF.Identity, bias=dtb_sb[:, :]),
                                     r=["psB", "dtb"], w=["dt_b"])
                            else:
                                p.op("v", lambda e: e.engine_nop(), w=["dt_b"])
                        else:
                            p.op("v", lambda e: e.tensor_copy(xbc_sb[:, 0:PSA_W], psA[:, 0:PSA_W]),
                                 r=PSA01, w=["xbc0"])
                            if TH > 2048:
                                p.op("v", lambda e: e.tensor_copy(xbc_sb[:, 2048:TH], psB[:, 0:TH - 2048]),
                                     r=["psB"], w=["xbc1"])
                            else:
                                p.op("v", lambda e: e.engine_nop(), w=["xbc1"])
                            cbase = tl * KCONV * 128
                            cb_ = cdg_b[ci % 2]
                            rounds = colsplits(T, 1024)
                            for ri, (r0, rw) in enumerate(rounds):
                                fs = []
                                for (s0, sw) in colsplits(rw, 512):
                                    for tap in range(KCONV):
                                        fs.append(lambda e, r0=r0, s0=s0, sw=sw, tap=tap, cbase=cbase, cb_=cb_: e.matmul(
                                            psB[:, s0:s0 + sw],
                                            cb_[:, cbase + tap * 128: cbase + (tap + 1) * 128],
                                            xbc_sb[:, r0 + s0 + tap: r0 + s0 + tap + sw],
                                            start=(tap == 0), stop=(tap == KCONV - 1)))
                                p.ops("t", fs, r=["xbc0", "xbc1", creg], w=["psB"])
                                p.op("s", lambda e, r0=r0, rw=rw, ctl=ctl: e.activation(
                                    xs_out[:, r0:r0 + rw], psB[:, 0:rw], AF.Silu, bias=cb_sb[:, ctl:ctl + 1]),
                                    r=["psB", "cb"], w=[f"xsout{ri}"])
                            xsregs = [f"xsout{r2}" for r2 in range(len(rounds))]
                            if ctl < N_XS_T:
                                p.dma("s", "xsp", d_xs[ts(ctl), :], xs_out[:, :], r=xsregs, w=[f"dxs{ctl}"])
                            elif ctl == N_XS_T:
                                p.op("v", lambda e: e.tensor_copy(B_sb[:, :], xs_out[:, :]), r=xsregs, w=["B_sb"])
                            else:
                                p.op("v", lambda e: e.tensor_copy(C_sb[:, :], xs_out[:, :]), r=xsregs, w=["C_sb"])
            if dbg:
                p.join()
                for ctl in range(N_XS_T):
                    p.dma("g", "dbgo2", dbg_t["dbg_xs"][ts(ctl), :], d_xs[ts(ctl), :], r=[f"dxs{ctl}"], w=["dbg2"])
                p.dma("s", "dbgo", dbg_t["dbg_dt"][:, :], dt_sb[0:HEADS, :], r=["dt_a", "dt_b"], w=["dbg3"])

            p.join()
            with contextlib.ExitStack() as p2c:
                E2 = p2c.enter_context
                wg_b = [E2(nc.sbuf_tensor(f"wg_b{i}", [128, NKT * 1024], BF16)) for i in range(2)]
                gsil = E2(nc.sbuf_tensor("gsil", [128, 1024], BF16))

                def load_gchunk(gi):
                    p.dma("g", f"wgc{gi % 2}", bass.AP(wg_b[gi % 2], 0, [[NKT * 1024, 128], [1024, NKT], [1, 1024]]),
                          bass.AP(WpT, gi * 1024, [[PROJ, 128], [128 * PROJ, NKT], [1, 1024]]),
                          w=[f"wgb{gi % 2}"])

                load_gchunk(0)
                for gi in range(INTER // 1024):
                    if gi + 1 < INTER // 1024:
                        load_gchunk(gi + 1)
                    wreg = f"wgb{gi % 2}"
                    wgb = wg_b[gi % 2]
                    for tt in range(NTT_R):
                        bank = (tt % 2) * 1024
                        fs = []
                        for k in range(NKT):
                            for half in range(2):
                                fs.append(lambda e, k=k, half=half, tt=tt, wgb=wgb, bank=bank: e.matmul(
                                    psA[:, bank + half * 512: bank + (half + 1) * 512],
                                    x_nT[:, k * TH + 3 + tt * 128: k * TH + 3 + (tt + 1) * 128],
                                    wgb[:, k * 1024 + half * 512: k * 1024 + (half + 1) * 512],
                                    start=(k == 0), stop=(k == NKT - 1)))
                        p.ops("t", fs, r=[wreg] + XNT_ALL, w=[f"psA{tt % 2}"])
                        p.op("s", lambda e, bank=bank: e.activation(gsil[:, :], psA[:, bank:bank + 1024], AF.Silu),
                             r=[f"psA{tt % 2}"], w=["gsil"])
                        p.dma("s", "gsp", d_gate[ts(tt), gi * 1024:(gi + 1) * 1024], gsil[:, :],
                              r=["gsil"], w=[f"dgate{gi}_{tt}"])
            GATE_ALL = [f"dgate{gi}_{tt}" for gi in range(INTER // 1024) for tt in range(NTT_R)]
            if dbg:
                p.join()
                for tt in range(NTT_R):
                    p.dma("g", "dbgo2", dbg_t["dbg_gate"][ts(tt), :], d_gate[ts(tt), :], r=GATE_ALL, w=["dbg4"])

        _stop('P2')
        p.join()
        DT_ALL = ["dt_a", "dt_b"]
        p.op("s", lambda e: e.activation(smld_sb[:, :], dt_sb[:, :], AF.Abs),
             r=DT_ALL, w=["smld"])
        p.op("s", lambda e: e.activation(dtA_sb[:, :], smld_sb[:, :], AF.Exp, scale=-1.0),
             r=["smld"], w=["dtA"])
        p.op("s", lambda e: e.activation(smld_sb[:, :], dtA_sb[:, :], AF.Ln, bias=cst[0:HEADS, 2:3]),
             r=["dtA", "cst1"], w=["smld"])
        p.op("v", lambda e: e.scalar_tensor_tensor(dt_sb[:, :], dt_sb[:, :], 0.0, smld_sb[:, :],
                                                   ALU.max, ALU.add),
             r=DT_ALL + ["smld"], w=["dt_a", "dt_b"])
        p.op("v", lambda e: e.tensor_scalar_mul(dtA_sb[:, :], dt_sb[:, :], negA[:, 0:1]),
             r=DT_ALL + ["negA", "dtA"], w=["dtA"])
        p.op("v", lambda e: e.reduce_sum(ld_sb[:, :], dtA_sb[:, :], axis=mybir.AxisListType.X),
             r=["dtA"], w=["ld"])
        _stop('P3s')
        for kc in range(NCH):
            p.op("v", lambda e, kc=kc: e.tensor_tensor_scan(s_sb[:, kc * Q:(kc + 1) * Q], onesrow64[:, :],
                                                            dtA_sb[:, kc * Q:(kc + 1) * Q], 0.0, ALU.mult, ALU.add),
                 r=["dtA", "ones64"], w=[f"s{kc}"])
        S_ALL = [f"s{kc}" for kc in range(NCH)]
        _stop('P3t')
        p.op("s", lambda e: e.activation(dtA_sb[:, :], dt_sb[:, :], AF.Ln), r=DT_ALL + ["dtA", "ld"], w=["dtA"])
        p.op("v", lambda e: e.tensor_tensor(smld_sb[:, :], s_sb[:, :], dtA_sb[:, :], ALU.subtract),
             r=S_ALL + ["dtA"], w=["smld"])
        _stop('P3u')
        for kc in range(NCH):
            p.ops("t", [
                lambda e, kc=kc: e.transpose(psB[:, 0:64], s_sb[0:HEADS, kc * Q:(kc + 1) * Q], ident_f[0:HEADS, 0:HEADS]),
                lambda e, kc=kc: e.transpose(psB[:, 64:128], smld_sb[0:HEADS, kc * Q:(kc + 1) * Q],
                                             ident_f[0:HEADS, 0:HEADS]),
            ], r=S_ALL + ["smld", "ident_f"], w=["psB"])
            p.op("s", lambda e, kc=kc: e.activation(sexpT[:, kc * HEADS:(kc + 1) * HEADS], psB[:, 0:64], AF.Exp),
                 r=["psB"], w=[f"sexpT{kc}"])
            p.op("v", lambda e, kc=kc: e.tensor_copy(smldT[:, kc * HEADS:(kc + 1) * HEADS], psB[:, 64:128]),
                 r=["psB"], w=[f"smldT{kc}"])
        for kc in range(NCH):
            p.dma("y", "srsp", bass.AP(d_srow, kc * HEADS * Q, [[Q, HEADS], [1, Q]]),
                  s_sb[0:HEADS, kc * Q:(kc + 1) * Q], r=S_ALL, w=[f"dsrow{kc}"])
        if dbg:
            p.dma("s", "dbgo", dbg_t["dbg_s"][:, :], s_sb[0:HEADS, :], r=S_ALL, w=["dbgs"])
        _stop('P3p')

        with contextlib.ExitStack() as P34:
            E3o = P34.enter_context
            S_runT = E3o(nc.sbuf_tensor("S_runT", [128, INTER], F32))
            enterT_state = E3o(nc.sbuf_tensor("enterT_state", [128, INTER], F32))
            dche = E3o(nc.sbuf_tensor("dche", [HEADS, NCH], F32))

            with contextlib.ExitStack() as p3p:
                pexp2 = p3p.enter_context(nc.sbuf_tensor("pexp2", [HEADS, INTER], F32))
                p.dma("y", "cst", pexp2[:, :], pexp_in[:, :], w=["pexp2"])
                p.op("v", lambda e: e.tensor_copy(dche[:, :], bass.AP(s_sb, Q - 1, [[T, HEADS], [Q, NCH]])),
                     r=S_ALL, w=["dche"])
                p.op("s", lambda e: e.activation(dche[:, :], dche[:, :], AF.Exp), r=["dche"], w=["dche"])
                fs = []
                for t in range(32):
                    fs.append(lambda e, t=t: e.matmul(psA[:, t * NCH:(t + 1) * NCH], pexp2[:, t * 128:(t + 1) * 128],
                                                      dche[:, :], start=True, stop=True))
                p.ops("t", fs, r=["pexp2", "dche"], w=["psA0"])
                p.op("v", lambda e: e.tensor_copy(dexp[:, 0:32 * NCH], psA[:, 0:32 * NCH]), r=["psA0"], w=["dexp"])

            p.join()
            _stop('P3a')
            with contextlib.ExitStack() as p3:
                E3 = p3.enter_context
                srow_b = [E3(nc.sbuf_tensor("srow0", [1, HEADS * Q], F32))]
                xsch = [E3(nc.sbuf_tensor(f"xsch{i}", [128, INTER], BF16)) for i in range(2)]
                xsT_ch = [E3(nc.sbuf_tensor(f"xsT{i}", [128, INTER], BF16)) for i in range(2)]
                Ech = [E3(nc.sbuf_tensor(f"Ech{i}", [128, 1024], F32)) for i in range(2)]
                Eexp = [E3(nc.sbuf_tensor(f"Eexp{i}", [128, 1024], F32)) for i in range(2)]
                Pm_sb = E3(nc.sbuf_tensor("Pm_sb", [128, 128], F32))
                BT_sb = E3(nc.sbuf_tensor("BT_sb", [128, 128], BF16))
                M_sb = [E3(nc.sbuf_tensor(f"M_sb{i}", [128, 1024], BF16)) for i in range(2)]
                tmpv = E3(nc.sbuf_tensor("tmpv", [128, 512], F32))
                ypartb = E3(nc.sbuf_tensor("ypartb", [128, INTER], BF16))
                xsw_sb = E3(nc.sbuf_tensor("xsw_sb", [128, 512], BF16))
                slocf = E3(nc.sbuf_tensor("slocf", [128, 512], F32))

                def load_xs(kc):
                    p.dma("y", f"xsr{kc % 2}", bass.AP(xsch[kc % 2], 0, [[INTER, 128], [128, 32], [1, 128]]),
                          bass.AP(d_xs, kc * 128, [[T, 128], [128 * T, 32], [1, 128]]),
                          r=[f"dxs{ctl}" for ctl in range(N_XS_T)], w=[f"xsch{kc % 2}"])

                def load_srow(kc):
                    p.dma("y", "srr0", srow_b[0][0:1, :],
                          bass.AP(d_srow, kc * HEADS * Q, [[1, 1], [1, HEADS * Q]]),
                          r=[f"dsrow{kc}"], w=["srowb0"])

                load_xs(0)
                for kc in range(NCH):
                    load_srow(kc)
                    if kc + 1 < NCH:
                        load_xs(kc + 1)
                    xr = f"xsch{kc % 2}"
                    xb = xsch[kc % 2]
                    xT = xsT_ch[kc % 2]
                    xTreg = f"xsT{kc % 2}"
                    for g in range(4):
                        p.ops("t", [(lambda e, j=j, g=g, xb=xb: e.transpose(psT[:, j * 128:(j + 1) * 128],
                                                                           xb[:, (g * 8 + j) * 128:(g * 8 + j + 1) * 128],
                                                                           ident_b[:, :]))
                                    for j in range(8)], r=[xr, "ident_b"], w=["psT"])
                        p.op("v", lambda e, g=g, xT=xT: e.tensor_copy(xT[:, g * 1024:(g + 1) * 1024], psT[:, :]),
                             r=["psT"], w=[xTreg])
                    p.ops("t", [lambda e, kc=kc: e.matmul(psB[:, 0:128], B_sb[:, kc * Q:(kc + 1) * Q],
                                                          C_sb[:, kc * Q:(kc + 1) * Q], start=True, stop=True)],
                          r=["B_sb", "C_sb"], w=["psB0"])
                    p.op("v", lambda e: e.tensor_tensor(Pm_sb[:, :], psB[:, 0:128], tri01[:, :], ALU.mult),
                         r=["psB0", "tri01"], w=["Pm"])
                    p.ops("t", [lambda e, kc=kc: e.transpose(psT2[:, 0:128], B_sb[:, kc * Q:(kc + 1) * Q], ident_b[:, :])],
                          r=["B_sb", "ident_b"], w=["psT2"])
                    p.op("v", lambda e: e.tensor_copy(BT_sb[:, :], psT2[:, 0:128]), r=["psT2"], w=["BT"])
                    srw = srow_b[0]
                    srwreg = "srowb0"

                    def emit_bc(hg):
                        pa = hg % 2
                        bb = pa * 1024
                        fs = []
                        for half in range(2):
                            fs.append(lambda e, hg=hg, half=half, bb=bb: e.matmul(
                                psA[:, bb + half * 512:bb + (half + 1) * 512],
                                ones1f[:, :], srw[0:1, hg * 1024 + half * 512:hg * 1024 + (half + 1) * 512],
                                start=True, stop=True))
                        p.ops("t", fs, r=["ones1f", srwreg], w=[f"psA{pa}"])

                    emit_bc(0)
                    for hg in range(8):
                        pa = hg % 2
                        bb = pa * 1024
                        Ech_ = Ech[pa]
                        Eexp_ = Eexp[pa]
                        M_ = M_sb[pa]
                        fs = []
                        for hh in range(8):
                            h = hg * 8 + hh
                            slot = bb + hh * 128
                            fs.append(lambda e, h=h, hh=hh, slot=slot, kc=kc, Ech_=Ech_: e.scalar_tensor_tensor(
                                Ech_[:, hh * 128:(hh + 1) * 128], psA[:, slot:slot + 128],
                                smldT[:, kc * HEADS + h:kc * HEADS + h + 1],
                                bass.AP(cst, 1, [[8, 128], [0, 128]]), ALU.subtract, ALU.min))
                        p.ops("v", fs, r=[f"psA{pa}", f"smldT{kc}", "cst87"], w=[f"Ech{pa}"])
                        if hg + 1 < 8:
                            emit_bc(hg + 1)
                        p.op("s", lambda e, Ech_=Ech_, Eexp_=Eexp_: e.activation(Eexp_[:, :], Ech_[:, :], AF.Exp),
                             r=[f"Ech{pa}"], w=[f"Eexp{pa}"])
                        p.ops("v", [(lambda e, hh=hh, Eexp_=Eexp_, M_=M_: e.tensor_tensor(
                            M_[:, hh * 128:(hh + 1) * 128],
                            Eexp_[:, hh * 128:(hh + 1) * 128], Pm_sb[:, :], ALU.mult)) for hh in range(4)],
                              r=[f"Eexp{pa}", "Pm"], w=[f"Ma{pa}"])
                        p.ops("g", [(lambda e, hh=hh, Eexp_=Eexp_, M_=M_: e.tensor_tensor(
                            M_[:, hh * 128:(hh + 1) * 128],
                            Eexp_[:, hh * 128:(hh + 1) * 128], Pm_sb[:, :], ALU.mult)) for hh in range(4, 8)],
                              r=[f"Eexp{pa}", "Pm"], w=[f"Mb{pa}"])
                        fs = []
                        for hh in range(8):
                            h = hg * 8 + hh
                            fs.append(lambda e, h=h, hh=hh, xT=xT, M_=M_: e.matmul(
                                psB[:, hh * 64:(hh + 1) * 64], M_[:, hh * 128:(hh + 1) * 128],
                                xT[:, h * 64:(h + 1) * 64], start=True, stop=True))
                        p.ops("t", fs, r=[f"Ma{pa}", f"Mb{pa}", xTreg], w=["psB0"])
                        p.op("v", lambda e, hg=hg, xT=xT: e.tensor_tensor(
                            tmpv[:, :], xT[:, hg * 512:(hg + 1) * 512],
                            bass.AP(drep, hg * 8, [[HEADS, 128], [1, 8], [0, HEADDIM]]), ALU.mult),
                            r=[xTreg, "drep"], w=["tmpv"])
                        p.op("v", lambda e, hg=hg: e.tensor_tensor(ypartb[:, hg * 512:(hg + 1) * 512], tmpv[:, :],
                                                                   psB[:, 0:512], ALU.add),
                             r=["tmpv", "psB0"], w=[f"ypb{hg}"])
                        p.op("g", lambda e, hg=hg, xT=xT, Eexp_=Eexp_: e.tensor_tensor(
                            xsw_sb[:, :], xT[:, hg * 512:(hg + 1) * 512],
                            bass.AP(Eexp_, Q - 1, [[1024, 128], [128, 8], [0, HEADDIM]]), ALU.mult),
                            r=[xTreg, f"Eexp{pa}"], w=["xsw"])
                        fs = []
                        for pr in range(4):
                            fs.append(lambda e, pr=pr: e.matmul(
                                psB[0:128, 512 + pr * 128:512 + (pr + 1) * 128],
                                xsw_sb[:, pr * 128:(pr + 1) * 128], BT_sb[:, :],
                                start=True, stop=True))
                        p.ops("t", fs, r=["xsw", "BT"], w=["psB1"])
                        p.op("s", lambda e: e.activation(slocf[:, :], psB[:, 512:1024], AF.Identity),
                             r=["psB1"], w=["slocf"])
                        p.dma("g", "slsp", d_sloc[kc * 128:(kc + 1) * 128, hg * 512:(hg + 1) * 512], slocf[:, :],
                              r=["slocf"], w=[f"dsl{kc}_{hg}"])
                        if kc > 0:
                            p.op("g", lambda e, hg=hg, kc=kc: e.tensor_tensor(
                                S_runT[:, hg * 512:(hg + 1) * 512], S_runT[:, hg * 512:(hg + 1) * 512],
                                bass.AP(dexp, (hg * 4) * NCH + kc, [[32 * NCH, 128], [NCH, 4], [0, 128]]),
                                ALU.mult),
                                 r=["dexp", "S_run"], w=["S_run"])
                            p.op("g", lambda e, hg=hg: e.tensor_tensor(
                                S_runT[:, hg * 512:(hg + 1) * 512], S_runT[:, hg * 512:(hg + 1) * 512],
                                slocf[:, :], ALU.add),
                                 r=["slocf", "S_run"], w=["S_run"])
                        else:
                            p.op("g", lambda e, hg=hg: e.tensor_copy(
                                S_runT[:, hg * 512:(hg + 1) * 512], slocf[:, :]),
                                 r=["slocf"], w=["S_run"])
                    p.dma("s", "ypsp", d_ypart[kc * 128:(kc + 1) * 128, :], ypartb[:, :],
                          r=[f"ypb{g}" for g in range(8)], w=[f"dyp{kc}"])
                    if kc == 0:
                        _stop('P3c0')
            p.join()
            _stop('P3')
            p.dma("s", "stp", bass.AP(ag_st_in, 0, [[STN, 1], [1, 128 * INTER]]),
                  bass.AP(S_runT, 0, [[INTER, 128], [1, INTER]]), r=["S_run"], w=["agst_a"])
            p.dma("s", "stp", ag_st_in[0:1, 128 * INTER:STN], ld_sb[0:HEADS, 0:1], r=["ld"], w=["agst_b"])
            p.collective(bass.AP(ag_st_in, 0, [[1, 1], [1, STN]]),
                         bass.AP(ag_st_out, 0, [[1, 1], [1, NCORES * STN]]),
                         r=["agst_a", "agst_b"], w=["AGST"])
            if dbg:
                p.dma("s", "dbgo", dbg_t["dbg_srun"][:, :], S_runT[:, :], r=["S_run"], w=["dbgsr"])
                for kc in range(NCH):
                    p.dma("g", "dbgo2", dbg_t["dbg_ypart"][kc * 128:(kc + 1) * 128, :],
                          d_ypart[kc * 128:(kc + 1) * 128, :], r=[f"dyp{kc}"], w=["dbgyp"])

            p.join()
            with contextlib.ExitStack() as p4:
                E4 = p4.enter_context
                pexp3 = E4(nc.sbuf_tensor("pexp3", [HEADS, INTER], F32))
                ldall = E4(nc.sbuf_tensor("ldall", [9, HEADS], F32))
                lwa = E4(nc.sbuf_tensor("lwa", [NCORES, HEADS], F32))
                lwaT = E4(nc.sbuf_tensor("lwaT", [HEADS, NCORES], F32))
                cexp = E4(nc.sbuf_tensor("cexp", [128, 32 * NCORES], F32))
                srbuf = [E4(nc.sbuf_tensor(f"srbuf{i}", [128, INTER], F32)) for i in range(2)]
                p.dma("y", "cst", pexp3[:, :], pexp_in[:, :], w=["pexp3"])
                p.op("v", lambda e: e.memset(ldall[:, :], 1.0), w=["ldall1"])
                p.dma("y", "ldl", ldall[0:NCORES, :], bass.AP(ag_st_out, 128 * INTER, [[STN, NCORES], [1, HEADS]]),
                      r=["AGST", "ldall1"], w=["ldall0"])
                p.ops("t", [lambda e: e.matmul(psB[0:NCORES, 0:HEADS], m2l_sb[:, :], ldall[:, :],
                                               start=True, stop=True)],
                      r=["m2l", "ldall0", "ldall1"], w=["psB"])
                p.op("v", lambda e: e.tensor_copy(lwa[:, :], psB[0:NCORES, 0:HEADS]), r=["psB"], w=["lwa"])
                p.ops("t", [lambda e: e.transpose(psB[0:HEADS, 128:128 + NCORES], lwa[:, :], ident_f[0:NCORES, 0:NCORES])],
                      r=["lwa", "ident_f"], w=["psB"])
                p.op("v", lambda e: e.tensor_copy(lwaT[:, :], psB[0:HEADS, 128:128 + NCORES]), r=["psB"], w=["lwaT"])
                fs = []
                for t in range(32):
                    fs.append(lambda e, t=t: e.matmul(psA[:, t * NCORES:(t + 1) * NCORES],
                                                      pexp3[:, t * 128:(t + 1) * 128], lwaT[:, :],
                                                      start=True, stop=True))
                p.ops("t", fs, r=["pexp3", "lwaT"], w=["psA0"])
                p.op("s", lambda e: e.activation(cexp[:, :], psA[:, 0:32 * NCORES], AF.Exp), r=["psA0"], w=["cexp"])

                def load_sr(r):
                    p.dma("y", f"srl{r % 2}", srbuf[r % 2][:, :],
                          bass.AP(ag_st_out, r * STN, [[INTER, 128], [1, INTER]]),
                          r=["AGST"], w=[f"srbuf{r % 2}"])

                load_sr(0)
                for r in range(NCORES):
                    if r + 1 < NCORES:
                        load_sr(r + 1)
                    sb_ = srbuf[r % 2]
                    sreg = f"srbuf{r % 2}"
                    for t4 in range(32):
                        if r == 0:
                            p.op("v", lambda e, t4=t4, sb_=sb_, r=r: e.tensor_scalar_mul(
                                enterT_state[:, t4 * 128:(t4 + 1) * 128], sb_[:, t4 * 128:(t4 + 1) * 128],
                                cexp[:, t4 * NCORES + r:t4 * NCORES + r + 1]),
                                r=[sreg, "cexp"], w=["enter"])
                        else:
                            p.op("v", lambda e, t4=t4, sb_=sb_, r=r: e.scalar_tensor_tensor(
                                enterT_state[:, t4 * 128:(t4 + 1) * 128], sb_[:, t4 * 128:(t4 + 1) * 128],
                                cexp[:, t4 * NCORES + r:t4 * NCORES + r + 1],
                                enterT_state[:, t4 * 128:(t4 + 1) * 128], ALU.mult, ALU.add),
                                r=[sreg, "cexp", "enter"], w=["enter"])
            if dbg:
                p.join()
                p.dma("s", "dbgo", dbg_t["dbg_enter"][:, :], enterT_state[:, :], r=["enter"], w=["dbgen"])

            p.join()
            _stop('P4')
            with contextlib.ExitStack() as p5a:
                E5 = p5a.enter_context
                slocb = [E5(nc.sbuf_tensor(f"slocb{i}", [128, INTER], BF16)) for i in range(3)]
                ypr = [E5(nc.sbuf_tensor(f"ypr{i}", [128, INTER], BF16)) for i in range(2)]
                gtr = [E5(nc.sbuf_tensor(f"gtr{i}", [128, INTER], BF16)) for i in range(2)]
                enterT_b = E5(nc.sbuf_tensor("enterT_b", [128, INTER], BF16))
                entb_stage = E5(nc.sbuf_tensor("entb_stage", [128, INTER // 2], BF16))
                tmpv2 = E5(nc.sbuf_tensor("tmpv2", [128, 512], F32))
                ycha = E5(nc.sbuf_tensor("ycha", [128, INTER], F32))
                ychb = E5(nc.sbuf_tensor("ychb", [128, INTER], F32))
                gss = E5(nc.sbuf_tensor("gss", [128, 1], F32))
                grst = E5(nc.sbuf_tensor("grst", [128, 1], F32))
                ynb = E5(nc.sbuf_tensor("ynb", [128, INTER], BF16))
                ynt_sb = E5(nc.sbuf_tensor("ynt_sb", [128, 1024], BF16))

                def load_sloc(kc):
                    p.dma("g", f"slr{kc % 3}", slocb[kc % 3][:, :], d_sloc[kc * 128:(kc + 1) * 128, :],
                          r=[f"dsl{kc}_{hg}" for hg in range(8)], w=[f"slocb{kc % 3}"])

                def load_yp(kc):
                    p.dma("y", f"ypr{kc % 2}", ypr[kc % 2][:, :], d_ypart[kc * 128:(kc + 1) * 128, :],
                          r=[f"dyp{kc}"], w=[f"ypr{kc % 2}"])

                def load_gt(kc):
                    p.dma("y", f"gtr{kc % 2}", gtr[kc % 2][:, :], d_gate[kc * 128:(kc + 1) * 128, :],
                          r=GATE_ALL, w=[f"gtr{kc % 2}"])

                load_sloc(0)
                load_yp(0)
                load_gt(0)
                for kc in range(NCH):
                    if kc + 1 < NCH:
                        load_sloc(kc + 1)
                        load_yp(kc + 1)
                        load_gt(kc + 1)
                    if kc > 0:
                        sl = slocb[(kc - 1) % 3]
                        slreg = f"slocb{(kc - 1) % 3}"
                        p.op("g", lambda e, kc=kc: e.tensor_tensor(
                            enterT_state[:, :], enterT_state[:, :],
                            bass.AP(dexp, kc - 1, [[32 * NCH, 128], [NCH, 32], [0, 128]]), ALU.mult),
                            r=["dexp", "enter"], w=["enter"])
                        p.op("g", lambda e, sl=sl: e.tensor_tensor(
                            enterT_state[:, :], enterT_state[:, :], sl[:, :], ALU.add),
                            r=["enter", slreg], w=["enter"])
                    for half in range(2):
                        p.op("v", lambda e, half=half: e.tensor_copy(
                            entb_stage[:, :], enterT_state[:, half * 2048:(half + 1) * 2048]),
                             r=["enter"], w=["entbs"])
                        for g in range(2 * half, 2 * half + 2):
                            p.ops("t", [(lambda e, j=j, g=g, half=half: e.transpose(
                                psT[:, j * 128:(j + 1) * 128],
                                entb_stage[:, (g * 8 + j) * 128 - half * 2048:(g * 8 + j + 1) * 128 - half * 2048],
                                ident_b[:, :]))
                                for j in range(8)], r=["entbs", "ident_b"], w=["psT"])
                            p.op("v", lambda e, g=g: e.tensor_copy(enterT_b[:, g * 1024:(g + 1) * 1024], psT[:, :]),
                                 r=["psT"], w=["enterTb"])
                    ypreg = f"ypr{kc % 2}"
                    gtreg = f"gtr{kc % 2}"
                    for hg in range(8):
                        p.ops("t", [lambda e, hg=hg, kc=kc: e.matmul(
                            psA[:, 0:512], C_sb[:, kc * Q:(kc + 1) * Q],
                            enterT_b[:, hg * 512:(hg + 1) * 512], start=True, stop=True)],
                              r=["C_sb", "enterTb"], w=["psA0"])
                        p.op("v", lambda e, hg=hg, kc=kc: e.tensor_tensor(
                            tmpv2[:, :], psA[:, 0:512],
                            bass.AP(sexpT, kc * HEADS + hg * 8, [[HEADS * NCH, 128], [1, 8], [0, HEADDIM]]),
                            ALU.mult),
                            r=["psA0", f"sexpT{kc}"], w=["tmpv2"])
                        p.op("v", lambda e, hg=hg, kc=kc: e.tensor_tensor(
                            ycha[:, hg * 512:(hg + 1) * 512], tmpv2[:, :],
                            ypr[kc % 2][:, hg * 512:(hg + 1) * 512], ALU.add),
                            r=["tmpv2", ypreg], w=["ycha"])
                    if dbg:
                        p.dma("s", "dbgo", dbg_t["dbg_y"][kc * 128:(kc + 1) * 128, :], ycha[:, :],
                              r=["ycha"], w=["dbgy"])
                    p.op("v", lambda e, kc=kc: e.tensor_tensor(ychb[:, :], ycha[:, :], gtr[kc % 2][:, :], ALU.mult),
                         r=["ycha", gtreg], w=["ychb"])
                    p.op("s", lambda e: e.activation(ycha[:, :], ychb[:, :], AF.Square, accum_out=gss[:, :]),
                         r=["ychb"], w=["ycha", "gss"])
                    p.op("s", lambda e: e.activation(grst[:, :], gss[:, :], AF.Ln, bias=cst[:, 0:1], scale=1.0 / INTER),
                         r=["gss", "cst"], w=["grst"])
                    p.op("s", lambda e: e.activation(grst[:, :], grst[:, :], AF.Exp, scale=-0.5),
                         r=["grst"], w=["grst"])
                    p.op("v", lambda e: e.scalar_tensor_tensor(ynb[:, :], ychb[:, :], grst[:, 0:1], gnw_rep[:, :],
                                                               ALU.mult, ALU.mult),
                         r=["ychb", "grst", "gnw_rep"], w=["ynb"])
                    for g in range(4):
                        p.ops("t", [(lambda e, j=j, g=g: e.transpose(psT[:, j * 128:(j + 1) * 128],
                                                                     ynb[:, (g * 8 + j) * 128:(g * 8 + j + 1) * 128],
                                                                     ident_b[:, :]))
                                    for j in range(8)], r=["ynb", "ident_b"], w=["psT"])
                        p.op("v", lambda e: e.tensor_copy(ynt_sb[:, :], psT[:, :]), r=["psT"], w=["yntsb"])
                        p.dma("s", "yntw", bass.AP(d_ynt, (g * 8) * 128 * T + kc * 128, [[T, 128], [128 * T, 8], [1, 128]]),
                              ynt_sb[:, :], r=["yntsb"], w=[f"dynt{kc}_{g}"])
        YNT_ALL = [f"dynt{kc}_{g}" for kc in range(NCH) for g in range(4)]
        mstack.close()
        open_scopes.remove(mstack)

        p.join()
        _stop('P5a')
        TPG = min(512, T)
        NTG = T // TPG
        NTTG = TPG // 128
        with contextlib.ExitStack() as p5b:
            E5 = p5b.enter_context
            ynT_g = E5(nc.sbuf_tensor("ynT_g", [128, 32 * TPG], BF16))
            wo_sl = [E5(nc.sbuf_tensor(f"wo_sl{i}", [128, 32 * 128], BF16)) for i in range(2)]
            ochf = E5(nc.sbuf_tensor("ochf", [128, 512], F32))
            x2pre = E5(nc.sbuf_tensor("x2pre", [128, NTTG * HID], F32))
            xres = [E5(nc.sbuf_tensor(f"xres{i}", [128, HID], F32)) for i in range(2)]
            mcol = E5(nc.sbuf_tensor("mcol", [128, 1], F32))
            sscol2 = E5(nc.sbuf_tensor("sscol2", [128, 1], F32))
            vcol = E5(nc.sbuf_tensor("vcol", [128, 1], F32))
            rstd2 = E5(nc.sbuf_tensor("rstd2", [128, 1], F32))
            nbcol = E5(nc.sbuf_tensor("nbcol", [128, 1], F32))
            sqscr = E5(nc.sbuf_tensor("sqscr", [128, HID], F32))
            x2til = E5(nc.sbuf_tensor("x2til", [128, HID], F32))
            tq = E5(nc.sbuf_tensor("tq", [128, HID], F32))
            uq = E5(nc.sbuf_tensor("uq", [128, HID], I16))
            c1q = E5(nc.sbuf_tensor("c1q", [128, HID], BF16))
            c2q = E5(nc.sbuf_tensor("c2q", [128, HID], BF16))
            qtmp = E5(nc.sbuf_tensor("qtmp", [128, HID], F32))
            x2q = E5(nc.sbuf_tensor("x2q", [128, HID], BF16))

            def load_wo(et):
                for kg in range(4):
                    p.dma("g", f"wol{et % 2}",
                          bass.AP(wo_sl[et % 2], kg * 8 * 128, [[32 * 128, 128], [128, 8], [1, 128]]),
                          bass.AP(WoT, (kg * 8 * 128) * HID + et * 128, [[HID, 128], [128 * HID, 8], [1, 128]]),
                          w=[f"wosl{et % 2}"])

            for tg in range(NTG):
                p.dma("y", "yntl", bass.AP(ynT_g, 0, [[32 * TPG, 128], [TPG, 32], [1, TPG]]),
                      bass.AP(d_ynt, tg * TPG, [[T, 128], [128 * T, 32], [1, TPG]]),
                      r=YNT_ALL, w=["ynTg"])
                if True:
                    load_wo(0)
                for et in range(NKT):
                    if et + 1 < NKT:
                        load_wo(et + 1)
                    slot = (et % 2) * 1024
                    pk = f"psA{et % 2}"
                    woreg = f"wosl{et % 2}"
                    wos = wo_sl[et % 2]
                    fs = []
                    for k in range(32):
                        fs.append(lambda e, k=k, slot=slot, wos=wos: e.matmul(
                            psA[:, slot:slot + TPG], wos[:, k * 128:(k + 1) * 128],
                            ynT_g[:, k * TPG:(k + 1) * TPG], start=(k == 0), stop=(k == 31)))
                    p.ops("t", fs, r=[woreg, "ynTg"], w=[pk])
                    p.op("v", lambda e, slot=slot: e.tensor_copy(ochf[:, 0:TPG], psA[:, slot:slot + TPG]),
                         r=[pk], w=["ochf"])
                    p.ops("t", [(lambda e, j=j: e.transpose(psB[:, j * 128:(j + 1) * 128],
                                                            ochf[:, j * 128:(j + 1) * 128], ident_f[:, :]))
                                for j in range(NTTG)], r=["ochf", "ident_f"], w=["psB"])
                    p.op("v", lambda e, et=et: e.tensor_copy(
                        bass.AP(x2pre, et * 128, [[NTTG * HID, 128], [HID, NTTG], [1, 128]]),
                        bass.AP(psB, 0, [[1024, 128], [128, NTTG], [1, 128]])),
                        r=["psB"], w=["x2pre"])
                for tt in range(NTTG):
                    ti = tg * NTTG + tt
                    bi = ti % 2
                    p.dma("y", f"xrl{bi}", xres[bi][:, :], x_sh[3 + ti * 128:3 + (ti + 1) * 128, :], w=[f"xres{bi}"])
                    blk = x2pre[:, tt * HID:(tt + 1) * HID]
                    p.op("v", lambda e, blk=blk, bi=bi: e.scalar_tensor_tensor(
                        blk, xres[bi][:, :], 2.0, blk, ALU.mult, ALU.add),
                         r=["x2pre", f"xres{bi}"], w=["x2pre"])
                    p.op("v", lambda e, blk=blk: e.reduce_sum(mcol[:, :], blk, axis=mybir.AxisListType.X),
                         r=["x2pre"], w=["mcol"])
                    p.op("v", lambda e: e.tensor_scalar_mul(mcol[:, :], mcol[:, :], 1.0 / HID),
                         r=["mcol"], w=["mcol"])
                    p.op("s", lambda e, blk=blk: e.activation(sqscr[:, :], blk, AF.Square, accum_out=sscol2[:, :]),
                         r=["x2pre"], w=["sqscr", "sscol2"])
                    p.op("v", lambda e: e.scalar_tensor_tensor(vcol[:, :], mcol[:, 0:1], 0.0, mcol[:, 0:1],
                                                               ALU.add, ALU.mult),
                         r=["mcol"], w=["vcol"])
                    p.op("v", lambda e: e.scalar_tensor_tensor(vcol[:, :], sscol2[:, 0:1], 1.0 / HID, vcol[:, 0:1],
                                                               ALU.mult, ALU.subtract),
                         r=["sscol2", "vcol"], w=["vcol"])
                    p.op("s", lambda e: e.activation(rstd2[:, :], vcol[:, :], AF.Ln, bias=cst[:, 0:1]),
                         r=["vcol", "cst"], w=["rstd2"])
                    p.op("s", lambda e: e.activation(rstd2[:, :], rstd2[:, :], AF.Exp, scale=-0.5),
                         r=["rstd2"], w=["rstd2"])
                    p.op("v", lambda e: e.tensor_tensor(nbcol[:, :], mcol[:, :], rstd2[:, :], ALU.mult),
                         r=["mcol", "rstd2"], w=["nbcol"])
                    p.op("v", lambda e: e.tensor_scalar_mul(nbcol[:, :], nbcol[:, :], -1.0),
                         r=["nbcol"], w=["nbcol"])
                    p.op("s", lambda e, blk=blk: e.activation(x2til[:, :], blk, AF.Identity, bias=nbcol[:, 0:1],
                                                              scale=rstd2[:, 0:1]),
                         r=["x2pre", "nbcol", "rstd2"], w=["x2til"])
                    p.dma("s", "x2sp", d_x2[ti * 128:(ti + 1) * 128, :], x2til[:, :], r=["x2til"], w=[f"dx2{ti}"])
                    p.op("v", lambda e: e.reduce_max(mcol[:, :], x2til[:, :], axis=mybir.AxisListType.X,
                                                     apply_absolute_value=True),
                         r=["x2til"], w=["mcol"])
                    p.op("v", lambda e: e.tensor_scalar_max(mcol[:, :], mcol[:, :], 1e-5), r=["mcol"], w=["mcol"])
                    p.op("v", lambda e, ti=ti: e.tensor_scalar_mul(x2inv[:, ti:ti + 1], mcol[:, :], 1.0 / 127.0),
                         r=["mcol"], w=["x2inv"])
                    p.op("v", lambda e: e.reciprocal(vcol[:, :], mcol[:, :]), r=["mcol"], w=["vcol"])
                    p.op("v", lambda e, ti=ti: e.tensor_scalar_mul(x2scl[:, ti:ti + 1], vcol[:, :], 127.0),
                         r=["vcol"], w=["x2scl"])
                    p.op("s", lambda e, ti=ti: e.activation(tq[:, :], x2til[:, :], AF.Identity,
                                                            scale=x2scl[:, ti:ti + 1]),
                         r=["x2til", "x2scl"], w=["tq"])
                    p.op("s", lambda e: e.copy(uq[:, :], tq[:, :]), r=["tq"], w=["uq"])
                    p.op("v", lambda e: e.scalar_tensor_tensor(c1q[:, :], uq[:, :], 0.5, tq[:, :],
                                                               ALU.add, ALU.is_le),
                         r=["uq", "tq"], w=["c1q"])
                    p.op("v", lambda e: e.scalar_tensor_tensor(c2q[:, :], uq[:, :], -0.5, tq[:, :],
                                                               ALU.add, ALU.is_ge),
                         r=["uq", "tq"], w=["c2q"])
                    p.op("v", lambda e: e.tensor_tensor(qtmp[:, :], uq[:, :], c1q[:, :], ALU.add),
                         r=["uq", "c1q"], w=["qtmp"])
                    p.op("v", lambda e: e.tensor_tensor(x2q[:, :], qtmp[:, :], c2q[:, :], ALU.subtract),
                         r=["qtmp", "c2q"], w=["x2q"])
                    for h in range(2):
                        p.ops("t", [(lambda e, j=j, h=h: e.transpose(psT[:, j * 128:(j + 1) * 128],
                                                                     x2q[:, (h * 8 + j) * 128:(h * 8 + j + 1) * 128],
                                                                     ident_b[:, :]))
                                    for j in range(8)], r=["x2q", "ident_b"], w=["psT"])
                        p.op("v", lambda e: e.tensor_copy(c1q[:, 0:1024], psT[:, :]), r=["psT"], w=["c1q"])
                        p.dma("s", "x2iw", bass.AP(d_x2i, (h * 8) * 128 * T + ti * 128,
                                                   [[T, 128], [128 * T, 8], [1, 128]]),
                              c1q[:, 0:1024], r=["c1q"], w=[f"dx2i{ti}_{h}"])
        X2I_ALL = [f"dx2i{ti}_{h}" for ti in range(NTT_R) for h in range(2)]
        if dbg:
            p.join()
            for ti in range(NTT_R):
                p.dma("g", "dbgo2", dbg_t["dbg_x2"][ti * 128:(ti + 1) * 128, :], d_x2[ti * 128:(ti + 1) * 128, :],
                      r=[f"dx2{ti}"], w=["dbgx2"])

        p.join()
        _stop('P5b')
        p.dma("g", "sct", d_tmp[0:1, 0:FFH], bass.AP(ag_sc_out, 0, [[W1SL + W2SL, NCORES], [1, W1SL]]),
              r=["AGSC"], w=["dtmpw1"])
        with contextlib.ExitStack() as p6:
            E6 = p6.enter_context
            x2iT = E6(nc.sbuf_tensor("x2iT", [128, NKT * T], BF16))
            w1sl = E6(nc.sbuf_tensor("w1sl", [128, NKT * 1024], BF16))
            wsrow = E6(nc.sbuf_tensor("wsrow", [1, 1024], F32))
            ws_sl = E6(nc.sbuf_tensor("ws_sl", [128, 1024], F32))
            h1 = E6(nc.sbuf_tensor("h1", [128, 1024], F32))
            h2 = E6(nc.sbuf_tensor("h2", [128, 1024], F32))
            sq6 = E6(nc.sbuf_tensor("sq6", [128, 1024], F32))
            m6 = E6(nc.sbuf_tensor("m6", [128, 1024], F32))
            u6 = E6(nc.sbuf_tensor("u6", [128, 1024], F32))
            g6 = E6(nc.sbuf_tensor("g6", [128, 1024], F32))
            hgel = E6(nc.sbuf_tensor("hgel", [128, 1024], F32))
            hmc = E6(nc.sbuf_tensor("hmc", [128, 1], F32))
            p.dma("y", "x2il", bass.AP(x2iT, 0, [[NKT * T, 128], [T, NKT], [1, T]]),
                  bass.AP(d_x2i, 0, [[T, 128], [128 * T, NKT], [1, T]]), r=X2I_ALL, w=["x2iT"])
            for hp in range(FFH // 1024):
                p.dma("y", "w1l", bass.AP(w1sl, 0, [[NKT * 1024, 128], [1024, NKT], [1, 1024]]),
                      bass.AP(ag_w1_out, hp * HID * W1SL, [[W1SL, 128], [128 * W1SL, NKT], [1, 1024]]),
                      r=["AGW1"], w=["w1sl"])
                p.dma("y", "wsr", wsrow[0:1, :], d_tmp[0:1, hp * 1024:(hp + 1) * 1024], r=["dtmpw1"], w=["wsrow"])
                p.ops("t", [(lambda e, i=i: e.matmul(psB[:, i * 512:(i + 1) * 512], ones1f[:, :],
                                                     wsrow[0:1, i * 512:(i + 1) * 512], start=True, stop=True))
                            for i in range(2)], r=["ones1f", "wsrow"], w=["psB"])
                p.op("v", lambda e: e.tensor_copy(ws_sl[:, :], psB[:, :]), r=["psB"], w=["ws_sl"])
                for tt in range(NTT_R):
                    slot = (tt % 2) * 1024
                    pk = f"psA{tt % 2}"
                    fs = []
                    for k in range(NKT):
                        for half2 in range(2):
                            fs.append(lambda e, k=k, half2=half2, tt=tt, slot=slot: e.matmul(
                                psA[:, slot + half2 * 512: slot + (half2 + 1) * 512],
                                x2iT[:, k * T + tt * 128:k * T + (tt + 1) * 128],
                                w1sl[:, k * 1024 + half2 * 512:k * 1024 + (half2 + 1) * 512],
                                start=(k == 0), stop=(k == NKT - 1)))
                    p.ops("t", fs, r=["w1sl", "x2iT"], w=[pk])
                    p.op("s", lambda e, slot=slot, tt=tt: e.activation(h1[:, :], psA[:, slot:slot + 1024],
                                                                       AF.Identity, scale=x2inv[:, tt:tt + 1]),
                         r=[pk, "x2inv"], w=["h1"])
                    p.op("g", lambda e: e.tensor_tensor(h2[:, :], h1[:, :], ws_sl[:, :], ALU.mult),
                         r=["h1", "ws_sl"], w=["h2"])
                    p.op("s", lambda e: e.activation(sq6[:, :], h2[:, :], AF.Square), r=["h2"], w=["sq6"])
                    p.op("v", lambda e: e.tensor_scalar(m6[:, :], sq6[:, :], GELU_B, GELU_A, ALU.mult, ALU.add),
                         r=["sq6"], w=["m6"])
                    p.op("g", lambda e: e.tensor_tensor(u6[:, :], m6[:, :], h2[:, :], ALU.mult),
                         r=["m6", "h2"], w=["u6"])
                    p.op("s", lambda e: e.activation(g6[:, :], u6[:, :], AF.Sigmoid), r=["u6"], w=["g6"])
                    p.op("v", lambda e: e.tensor_tensor(hgel[:, :], g6[:, :], h2[:, :], ALU.mult),
                         r=["g6", "h2"], w=["hgel"])
                    p.op("v", lambda e: e.reduce_max(hmc[:, :], hgel[:, :], axis=mybir.AxisListType.X,
                                                     apply_absolute_value=True),
                         r=["hgel"], w=["hmc"])
                    if hp == 0:
                        p.op("v", lambda e, tt=tt: e.tensor_copy(hmax_run[:, tt:tt + 1], hmc[:, :]),
                             r=["hmc"], w=["hmax"])
                    else:
                        p.op("v", lambda e, tt=tt: e.tensor_tensor(hmax_run[:, tt:tt + 1], hmax_run[:, tt:tt + 1],
                                                                   hmc[:, :], ALU.max),
                             r=["hmc", "hmax"], w=["hmax"])
                    p.dma("s", "hgsp", d_hg[tt * 128:(tt + 1) * 128, hp * 1024:(hp + 1) * 1024], hgel[:, :],
                          r=["hgel"], w=[f"dhg{hp}_{tt}"])
        HG_ALL = [f"dhg{hp}_{tt}" for hp in range(FFH // 1024) for tt in range(NTT_R)]
        if dbg:
            p.join()
            for tt in range(NTT_R):
                p.dma("g", "dbgo2", dbg_t["dbg_hg"][tt * 128:(tt + 1) * 128, :], d_hg[tt * 128:(tt + 1) * 128, :],
                      r=HG_ALL, w=["dbghg"])
            p.dma("g", "dbgo2", dbg_t["dbg_w1q"][:, :], ag_w1_out[:, :], r=["AGW1"], w=["dbgw1"])
            p.dma("g", "dbgo2", dbg_t["dbg_w2q"][:, :], ag_w2_out[:, :], r=["AGW2"], w=["dbgw2"])
            p.dma("g", "dbgo2", dbg_t["dbg_sc"][:, :], ag_sc_out[:, :], r=["AGSC"], w=["dbgsc"])

        p.join()
        _stop('P6')
        for tt in range(NTT_R):
            p.op("v", lambda e, tt=tt: e.tensor_scalar_max(hmax_run[:, tt:tt + 1], hmax_run[:, tt:tt + 1], 1e-5),
                 r=["hmax"], w=["hmax"])
        p.op("v", lambda e: e.tensor_scalar_mul(hinv[:, :], hmax_run[:, :], 1.0 / 127.0), r=["hmax"], w=["hinv"])
        p.op("v", lambda e: e.reciprocal(hscl[:, :], hmax_run[:, :]), r=["hmax"], w=["hscl"])
        p.op("v", lambda e: e.tensor_scalar_mul(hscl[:, :], hscl[:, :], 127.0), r=["hscl"], w=["hscl"])
        hvt = E(nc.sbuf_tensor("hvt", [NTT_R, 128], F32))
        p.ops("t", [lambda e: e.transpose(psB[0:NTT_R, 0:128], hinv[:, 0:NTT_R], ident_f[:, :])],
              r=["hinv", "ident_f"], w=["psB"])
        p.op("v", lambda e: e.tensor_copy(hvt[:, :], psB[0:NTT_R, 0:128]),
             r=["psB"], w=["hinvT"])
        p.dma("g", "sct", d_tmp[0:1, 0:T], bass.AP(hvt, 0, [[128, NTT_R], [1, 128]]), r=["hinvT"], w=["dtmph"])
        with contextlib.ExitStack() as p7:
            E7 = p7.enter_context
            hgr = [E7(nc.sbuf_tensor(f"hgr{i}", [128, 1024], F32)) for i in range(2)]
            tq7 = [E7(nc.sbuf_tensor(f"tq7{i}", [128, 1024], F32)) for i in range(2)]
            uq7 = [E7(nc.sbuf_tensor(f"uq7{i}", [128, 1024], I16)) for i in range(2)]
            c1q7 = [E7(nc.sbuf_tensor(f"c1q7{i}", [128, 1024], BF16)) for i in range(2)]
            c2q7 = [E7(nc.sbuf_tensor(f"c2q7{i}", [128, 1024], BF16)) for i in range(2)]
            qt7 = [E7(nc.sbuf_tensor(f"qt7{i}", [128, 1024], F32)) for i in range(2)]
            hq7 = [E7(nc.sbuf_tensor(f"hq7{i}", [128, 1024], BF16)) for i in range(2)]
            hts = [E7(nc.sbuf_tensor(f"hts{i}", [128, 1024], BF16)) for i in range(2)]
            idx = 0
            for hp in range(FFH // 1024):
                for tt in range(NTT_R):
                    bi = idx % 2
                    idx += 1
                    tq_, uq_, c1_, c2_, qt_, hq_, ht_ = (tq7[bi], uq7[bi], c1q7[bi], c2q7[bi],
                                                         qt7[bi], hq7[bi], hts[bi])
                    p.dma("y", f"hgl{bi}", hgr[bi][:, :], d_hg[tt * 128:(tt + 1) * 128, hp * 1024:(hp + 1) * 1024],
                          r=[f"dhg{hp}_{tt}"], w=[f"hgr{bi}"])
                    p.op("s", lambda e, bi=bi, tt=tt, tq_=tq_: e.activation(tq_[:, :], hgr[bi][:, :], AF.Identity,
                                                                            scale=hscl[:, tt:tt + 1]),
                         r=[f"hgr{bi}", "hscl"], w=[f"tq7{bi}"])
                    p.op("s", lambda e, tq_=tq_, uq_=uq_: e.copy(uq_[:, :], tq_[:, :]),
                         r=[f"tq7{bi}"], w=[f"uq7{bi}"])
                    p.op("v", lambda e, uq_=uq_, tq_=tq_, c1_=c1_: e.scalar_tensor_tensor(
                        c1_[:, :], uq_[:, :], 0.5, tq_[:, :], ALU.add, ALU.is_le),
                         r=[f"uq7{bi}", f"tq7{bi}"], w=[f"c1q7{bi}"])
                    p.op("v", lambda e, uq_=uq_, tq_=tq_, c2_=c2_: e.scalar_tensor_tensor(
                        c2_[:, :], uq_[:, :], -0.5, tq_[:, :], ALU.add, ALU.is_ge),
                         r=[f"uq7{bi}", f"tq7{bi}"], w=[f"c2q7{bi}"])
                    p.op("g", lambda e, uq_=uq_, c1_=c1_, qt_=qt_: e.tensor_tensor(
                        qt_[:, :], uq_[:, :], c1_[:, :], ALU.add),
                         r=[f"uq7{bi}", f"c1q7{bi}"], w=[f"qt7{bi}"])
                    p.op("g", lambda e, qt_=qt_, c2_=c2_, hq_=hq_: e.tensor_tensor(
                        hq_[:, :], qt_[:, :], c2_[:, :], ALU.subtract),
                         r=[f"qt7{bi}", f"c2q7{bi}"], w=[f"hq7{bi}"])
                    psT_ = psT if bi == 0 else psT2
                    psTreg = "psT" if bi == 0 else "psT2"
                    p.ops("t", [(lambda e, j=j, hq_=hq_, psT_=psT_: e.transpose(
                        psT_[:, j * 128:(j + 1) * 128],
                        hq_[:, j * 128:(j + 1) * 128], ident_b[:, :]))
                                for j in range(8)], r=[f"hq7{bi}", "ident_b"], w=[psTreg])
                    p.op("v", lambda e, ht_=ht_, psT_=psT_: e.tensor_copy(ht_[:, :], psT_[:, :]),
                         r=[psTreg], w=[f"hts{bi}"])
                    p.dma("s", "hiw", bass.AP(d_hint, (hp * 8) * 128 * T + tt * 128, [[T, 128], [128 * T, 8], [1, 128]]),
                          ht_[:, :], r=[f"hts{bi}"], w=[f"dhi{hp}_{tt}"])
        HI_ALL = [f"dhi{hp}_{tt}" for hp in range(FFH // 1024) for tt in range(NTT_R)]
        if dbg:
            p.join()
            for hp in range(FFH // 1024):
                p.dma("g", "dbgo2", dbg_t["dbg_hint"][hp * 1024:(hp + 1) * 1024, :],
                      d_hint[hp * 1024:(hp + 1) * 1024, :], r=HI_ALL, w=["dbghi"])

        p.join()
        _stop('P7')
        p.dma("g", "sct", d_tmp[0:1, 4096:4096 + HID],
              bass.AP(ag_sc_out, W1SL, [[W1SL + W2SL, NCORES], [1, W2SL]]), r=["AGSC"], w=["dtmpw2"])
        p.dma("y", "ws2l", ws2col[:, :], bass.AP(d_tmp, 4096, [[1, 128], [128, NKT]]), r=["dtmpw2"], w=["ws2c"], slow=True)
        with contextlib.ExitStack() as p8:
            E8 = p8.enter_context
            hslab = E8(nc.sbuf_tensor("hslab", [128, 64 * TPG], BF16))
            mcol2 = E8(nc.sbuf_tensor("mcol2", [128, 1], F32))
            ss8 = E8(nc.sbuf_tensor("ss8", [128, 1], F32))
            vc8 = E8(nc.sbuf_tensor("vc8", [128, 1], F32))
            rs8 = E8(nc.sbuf_tensor("rs8", [128, 1], F32))
            nb8 = E8(nc.sbuf_tensor("nb8", [128, 1], F32))
            w2sl = [E8(nc.sbuf_tensor(f"w2sl{i}", [128, 64 * 128], BF16)) for i in range(2)]
            s2xr = E8(nc.sbuf_tensor("s2xr", [128, T], F32))
            o2f = E8(nc.sbuf_tensor("o2f", [128, 512], F32))
            x3g = E8(nc.sbuf_tensor("x3g", [128, NTTG * HID], F32))
            x2r = [E8(nc.sbuf_tensor(f"x2r{i}", [128, HID], F32)) for i in range(2)]
            sq8 = E8(nc.sbuf_tensor("sq8", [128, HID], F32))
            fin = E8(nc.sbuf_tensor("fin", [128, HID], F32))
            grow8 = E8(nc.sbuf_tensor("grow8", [1, T], F32))
            p.dma("y", "wsr2", grow8[0:1, :], d_tmp[0:1, 0:T], r=["dtmph"], w=["grow8"])
            for (c0, w) in colsplits(T, 2048):
                fs = []
                for (s0, sw) in colsplits(w, 512):
                    fs.append(lambda e, c0=c0, s0=s0, sw=sw: e.matmul(psA[:, s0:s0 + sw], ones1f[:, :],
                                                                     grow8[0:1, c0 + s0:c0 + s0 + sw],
                                                                     start=True, stop=True))
                p.ops("t", fs, r=["ones1f", "grow8"], w=PSA01)
                p.op("v", lambda e, c0=c0, w=w: e.tensor_copy(s2xr[:, c0:c0 + w], psA[:, 0:w]),
                     r=PSA01, w=["s2xr"])

            def load_w2(et):
                rb = et // 2
                half = et % 2
                for kg in range(4):
                    p.dma("g", f"w2l{et % 2}",
                          bass.AP(w2sl[et % 2], kg * 16 * 128, [[64 * 128, 128], [128, 16], [1, 128]]),
                          bass.AP(ag_w2_out, (rb * FFH + kg * 16 * 128) * W2SL + half * 128,
                                  [[W2SL, 128], [128 * W2SL, 16], [1, 128]]),
                          w=[f"w2sl{et % 2}"])

            for tc in range(NTG):
                p.dma("y", "hil", bass.AP(hslab, 0, [[64 * TPG, 128], [TPG, 64], [1, TPG]]),
                      bass.AP(d_hint, tc * TPG, [[T, 128], [128 * T, 64], [1, TPG]]),
                      r=HI_ALL, w=["hslab"])
                if True:
                    load_w2(0)
                for et in range(NKT):
                    if et + 1 < NKT:
                        load_w2(et + 1)
                    slot = (et % 2) * 1024
                    pk = f"psA{et % 2}"
                    w2reg = f"w2sl{et % 2}"
                    w2s = w2sl[et % 2]
                    fs = []
                    for k in range(64):
                        fs.append(lambda e, k=k, slot=slot, w2s=w2s: e.matmul(
                            psA[:, slot:slot + TPG], w2s[:, k * 128:(k + 1) * 128],
                            hslab[:, k * TPG:(k + 1) * TPG], start=(k == 0), stop=(k == 63)))
                    p.ops("t", fs, r=[w2reg, "hslab"], w=[pk])
                    p.op("s", lambda e, slot=slot, et=et: e.activation(o2f[:, 0:TPG], psA[:, slot:slot + TPG],
                                                                       AF.Identity, scale=ws2col[:, et:et + 1]),
                         r=[pk, "ws2c"], w=["o2f"])
                    p.op("g", lambda e, tc=tc: e.tensor_tensor(o2f[:, 0:TPG], o2f[:, 0:TPG],
                                                               s2xr[:, tc * TPG:(tc + 1) * TPG], ALU.mult),
                         r=["o2f", "s2xr"], w=["o2f"])
                    p.ops("t", [(lambda e, j=j: e.transpose(psB[:, j * 128:(j + 1) * 128],
                                                            o2f[:, j * 128:(j + 1) * 128], ident_f[:, :]))
                                for j in range(NTTG)], r=["o2f", "ident_f"], w=["psB"])
                    p.op("v", lambda e, et=et: e.tensor_copy(
                        bass.AP(x3g, et * 128, [[NTTG * HID, 128], [HID, NTTG], [1, 128]]),
                        bass.AP(psB, 0, [[1024, 128], [128, NTTG], [1, 128]])),
                        r=["psB"], w=["x3g"])
                for tt in range(NTTG):
                    ti = tc * NTTG + tt
                    bi = ti % 2
                    p.dma("y", f"x2rl{bi}", x2r[bi][:, :], d_x2[ti * 128:(ti + 1) * 128, :],
                          r=[f"dx2{ti}"], w=[f"x2r{bi}"])
                    blk = x3g[:, tt * HID:(tt + 1) * HID]
                    p.op("v", lambda e, blk=blk, bi=bi: e.tensor_tensor(blk, blk, x2r[bi][:, :], ALU.add),
                         r=["x3g", f"x2r{bi}"], w=["x3g"])
                    if dbg:
                        p.dma("s", "dbgo", dbg_t["dbg_ff"][ti * 128:(ti + 1) * 128, :], blk,
                              r=["x3g"], w=["dbgff"])
                    p.op("v", lambda e, blk=blk: e.reduce_sum(mcol2[:, :], blk, axis=mybir.AxisListType.X),
                         r=["x3g"], w=["mcol2"])
                    p.op("v", lambda e: e.tensor_scalar_mul(mcol2[:, :], mcol2[:, :], 1.0 / HID),
                         r=["mcol2"], w=["mcol2"])
                    p.op("s", lambda e, blk=blk: e.activation(sq8[:, :], blk, AF.Square, accum_out=ss8[:, :]),
                         r=["x3g"], w=["sq8", "ss8"])
                    p.op("v", lambda e: e.scalar_tensor_tensor(vc8[:, :], mcol2[:, 0:1], 0.0, mcol2[:, 0:1],
                                                               ALU.add, ALU.mult), r=["mcol2"], w=["vc8"])
                    p.op("v", lambda e: e.scalar_tensor_tensor(vc8[:, :], ss8[:, 0:1], 1.0 / HID, vc8[:, 0:1],
                                                               ALU.mult, ALU.subtract), r=["ss8", "vc8"], w=["vc8"])
                    p.op("s", lambda e: e.activation(rs8[:, :], vc8[:, :], AF.Ln, bias=cst[:, 0:1]),
                         r=["vc8", "cst"], w=["rs8"])
                    p.op("s", lambda e: e.activation(rs8[:, :], rs8[:, :], AF.Exp, scale=-0.5), r=["rs8"], w=["rs8"])
                    p.op("v", lambda e: e.tensor_tensor(nb8[:, :], mcol2[:, :], rs8[:, :], ALU.mult),
                         r=["mcol2", "rs8"], w=["nb8"])
                    p.op("v", lambda e: e.tensor_scalar_mul(nb8[:, :], nb8[:, :], -1.0), r=["nb8"], w=["nb8"])
                    p.op("s", lambda e, blk=blk: e.activation(fin[:, :], blk, AF.Identity, bias=nb8[:, 0:1],
                                                              scale=rs8[:, 0:1]),
                         r=["x3g", "nb8", "rs8"], w=["fin"])
                    p.dma("s", "outw", out_y[ti * 128:(ti + 1) * 128, :], fin[:, :], r=["fin"], w=[f"outy{ti}"])
        p.join(skip_cc=False)
        p.emit_all()
        ctx.close()
    except _StopBuild:
        pass
    return nc, p, dbg_t


def _host_prep(inputs, T=2048):
    x = np.ascontiguousarray(inputs["L0_attn_hidden_states"], dtype=np.float32)
    Bb, S, _ = x.shape
    segs = NCORES // Bb
    WpT = np.ascontiguousarray(np.asarray(inputs["L0_attn_in_proj_w"], dtype=np.float32).T)
    WoT = np.ascontiguousarray(np.asarray(inputs["L0_attn_out_proj_w"], dtype=np.float32).T)
    w1T = np.ascontiguousarray(np.asarray(inputs["L0_ffn_w1"], dtype=np.float32).T)
    w2T = np.ascontiguousarray(np.asarray(inputs["L0_ffn_w2"], dtype=np.float32).T)
    conv_w = np.asarray(inputs["L0_attn_conv_w"], dtype=np.float32).reshape(CONVD, KCONV)
    convdiag = np.zeros((N_XBC_T, KCONV, 128, 128), dtype=np.float32)
    for ct in range(N_XBC_T):
        for tap in range(KCONV):
            np.fill_diagonal(convdiag[ct, tap], conv_w[ct * 128:(ct + 1) * 128, tap])
    convdiag = convdiag.reshape(N_XBC_T * KCONV * 128, 128)
    ident = np.eye(128, dtype=np.float32)
    tri01 = np.triu(np.ones((128, 128), dtype=np.float32))
    pexp = np.zeros((HEADS, INTER), dtype=np.float32)
    for hp in range(INTER):
        pexp[hp // HEADDIM, hp] = 1.0
    shared = dict(
        WpT=WpT, WoT=WoT, convdiag=convdiag,
        conv_b=np.ascontiguousarray(inputs["L0_attn_conv_b"], dtype=np.float32),
        nw_in=np.ascontiguousarray(inputs["L0_attn_norm_weight"], dtype=np.float32),
        gnw_in=np.ascontiguousarray(inputs["L0_attn_gate_norm_weight"], dtype=np.float32),
        dtb_in=np.ascontiguousarray(inputs["L0_attn_dt_bias"], dtype=np.float32),
        alog_in=np.ascontiguousarray(inputs["L0_attn_A_log"], dtype=np.float32),
        dd_in=np.ascontiguousarray(inputs["L0_attn_D"], dtype=np.float32),
        ident_in=ident, tri01_in=tri01, pexp_in=pexp,
    )
    in_maps = []
    for c in range(NCORES):
        b, q = c // segs, c % segs
        t0 = q * T
        xs = np.empty((T + 3, HID), dtype=np.float32)
        xs[3:] = x[b, t0:t0 + T]
        if q == 0:
            xs[:3] = 0.0
        else:
            xs[:3] = x[b, t0 - 3:t0]
        m2l = np.zeros((9, NCORES), dtype=np.float32)
        for r in range(NCORES):
            for t in range(NCORES):
                if r // segs == b and t // segs == b and (r % segs) < (t % segs) < q:
                    m2l[t, r] = 1.0
        for r in range(NCORES):
            ok = (r // segs == b) and (r % segs) < q
            m2l[8, r] = 0.0 if ok else -1e30
        m = dict(shared)
        m["x_sh"] = xs
        m["m2l_in"] = m2l
        m["w1T_sl"] = np.ascontiguousarray(w1T[:, c * (FFH // NCORES):(c + 1) * (FFH // NCORES)])
        m["w2T_sl"] = np.ascontiguousarray(w2T[:, c * (HID // NCORES):(c + 1) * (HID // NCORES)])
        in_maps.append(m)
    return in_maps


_CACHE = {}


def kernel(**inputs):
    T = 2048
    if "nc" not in _CACHE:
        nc, p, dbg_t = build(T, dbg=False)
        _CACHE["nc"] = nc
    in_maps = _host_prep(inputs, T)
    res = run_bass_kernel_spmd(_CACHE["nc"], in_maps, core_ids=list(range(NCORES)))
    x = inputs["L0_attn_hidden_states"]
    Bb, S, _ = np.asarray(x).shape
    segs = NCORES // Bb
    out = np.empty((Bb, S, HID), dtype=np.float32)
    for c in range(NCORES):
        out[c // segs, (c % segs) * T:(c % segs + 1) * T] = res.results[c]["out_y"]
    return out


# version 14
# speedup vs baseline: 1.0332x; 1.0332x over previous
import math
import contextlib
import numpy as np
import concourse.bass as bass
import concourse.mybir as mybir
from concourse.bass_utils import run_bass_kernel_spmd

F32 = mybir.dt.float32
BF16 = mybir.dt.bfloat16
I16 = mybir.dt.int16
AF = mybir.ActivationFunctionType
ALU = mybir.AluOpType

HID = 2048
STATE = 128
HEADS = 64
HEADDIM = 64
KCONV = 4
INTER = 4096
CONVD = INTER + 2 * STATE
PROJ = INTER + CONVD + HEADS
FFH = 8192
EPS = 1e-5
GELU_A = float(2.0 * math.sqrt(2.0 / math.pi))
GELU_B = float(2.0 * math.sqrt(2.0 / math.pi) * 0.044715)
NCORES = 8
Q = 128
NKT = HID // 128
N_XS_T = INTER // 128
N_XBC_T = CONVD // 128


class Prog:

    ENGS = ("v", "s", "t", "g", "y")

    def __init__(self, nc, ctx):
        self.nc = nc
        self.ctx = ctx
        E = ctx.enter_context
        self.sem = {k: E(nc.semaphore(f"pg_{k}")) for k in self.ENGS}
        self.cnt = {k: 0 for k in self.ENGS}
        self.dma_slots = {}
        self.cc_sem = E(nc.semaphore("cc_sem"))
        self.cc_cnt = 0
        self.last_w = {}
        self.readers = {}
        self.queues = {k: [] for k in self.ENGS}
        self.waited = {k: {} for k in self.ENGS}

    def _slot(self, name):
        if name not in self.dma_slots:
            self.dma_slots[name] = [self.ctx.enter_context(self.nc.semaphore(f"dq_{name}")), 0]
        return self.dma_slots[name]

    def _tok_sem(self, tok):
        kind = tok[0]
        if kind == "e":
            return ("pg_" + tok[1], self.sem[tok[1]], tok[2])
        if kind == "d":
            return ("dq_" + tok[1], self.dma_slots[tok[1]][0], 16 * self.dma_slots[tok[1]][1])
        return ("cc", self.cc_sem, tok[1])

    @staticmethod
    def _ex(regs):
        out = []
        for g in regs:
            if g == "psB":
                out.extend(("psB0", "psB1"))
            else:
                out.append(g)
        return out

    def _collect(self, ek, r, w, slots_out=None):
        r = self._ex(r)
        w = self._ex(w)
        waits = []
        seen = self.waited[ek]
        toks = []
        for reg in r:
            t = self.last_w.get(reg)
            if t is not None:
                toks.append(t)
            if reg.startswith("ps"):
                toks.extend(t2 for t2 in self.readers.get(reg, ())
                            if not (t2[0] == "e" and t2[1] == ek))
        for reg in w:
            t = self.last_w.get(reg)
            if t is not None:
                toks.append(t)
            toks.extend(self.readers.get(reg, ()))
        for t in toks:
            if t[0] == "d" and slots_out is not None:
                slots_out.add(t[1])
            key, sem, val = self._tok_sem(t)
            if seen.get(key, 0) < val:
                seen[key] = val
                waits.append((sem, val))
        return waits

    def _record(self, tok, r, w, slots=()):
        r = self._ex(r)
        w = self._ex(w)
        for reg in w:
            if reg.startswith("@"):
                self.readers[reg] = []
                continue
            self.last_w[reg] = tok
            self.readers[reg] = []
        for reg in r:
            self.readers.setdefault(reg, []).append(tok)
        for sname in slots:
            self.readers.setdefault("@" + sname, []).append(tok)

    def op(self, ek, f, r=(), w=()):
        sl = set()
        waits = self._collect(ek, r, w, sl)
        self.cnt[ek] += 1
        sem = self.sem[ek]

        def emit(e, waits=waits, f=f, sem=sem):
            for s, v in waits:
                e.wait_ge(s, v)
            f(e).then_inc(sem, 1)
        self.queues[ek].append(emit)
        self._record(("e", ek, self.cnt[ek]), r, w, sl)

    def ops(self, ek, fs, r=(), w=()):
        sl = set()
        waits = self._collect(ek, r, w, sl)
        self.cnt[ek] += 1
        sem = self.sem[ek]

        def emit(e, waits=waits, fs=tuple(fs), sem=sem):
            for s, v in waits:
                e.wait_ge(s, v)
            ins = None
            for f in fs:
                ins = f(e)
            ins.then_inc(sem, 1)
        self.queues[ek].append(emit)
        self._record(("e", ek, self.cnt[ek]), r, w, sl)

    def dma(self, ek, sname, out, in_, r=(), w=(), slow=False):
        slot = self._slot(sname)
        sl = set()
        waits = self._collect(ek, list(r), list(w) + ["@" + sname], sl)
        slot[1] += 1
        sem = slot[0]

        def emit(e, waits=waits, out=out, in_=in_, sem=sem, slow=slow):
            for s, v in waits:
                e.wait_ge(s, v)
            e.dma_start(out=out, in_=in_, allow_slow_non_contiguous=slow).then_inc(sem, 16)
        self.queues[ek].append(emit)
        self._record(("d", sname, slot[1]), list(r), list(w) + ["@" + sname], sl)

    def collective(self, ins_ap, outs_ap, r=(), w=()):
        sl = set()
        waits = self._collect("g", r, w, sl)
        self.cc_cnt += 1
        ccs = self.cc_sem

        def emit(e, waits=waits, ins_ap=ins_ap, outs_ap=outs_ap, ccs=ccs):
            for s, v in waits:
                e.wait_ge(s, v)
            e.collective_compute(
                "AllGather", ALU.bypass,
                replica_groups=[list(range(NCORES))],
                ins=[ins_ap], outs=[outs_ap], cc_dim="Partition",
            ).then_inc(ccs, 1)
        self.queues["g"].append(emit)
        self._record(("c", self.cc_cnt), r, w, sl)

    def join(self, skip_cc=True):
        toks = [("e", k, self.cnt[k]) for k in self.ENGS if self.cnt[k] > 0]
        toks += [("d", n, s[1]) for n, s in self.dma_slots.items() if s[1] > 0]
        if self.cc_cnt and not skip_cc:
            toks.append(("c", self.cc_cnt))
        for ek in self.ENGS:
            waits = []
            seen = self.waited[ek]
            for t in toks:
                if t[0] == "e" and t[1] == ek:
                    continue
                key, sem, val = self._tok_sem(t)
                if seen.get(key, 0) < val:
                    seen[key] = val
                    waits.append((sem, val))
            if waits:
                def emit(e, waits=waits):
                    for s, v in waits:
                        e.wait_ge(s, v)
                self.queues[ek].append(emit)

    def wait_region(self, ek, *regs):
        waits = self._collect(ek, regs, ())
        if waits:
            def emit(e, waits=waits):
                for s, v in waits:
                    e.wait_ge(s, v)
            self.queues[ek].append(emit)

    def emit_all(self):
        nc = self.nc
        with nc.Block() as block:
            @block.vector
            def _(e):
                for f in self.queues["v"]:
                    f(e)

            @block.scalar
            def _(e):
                for f in self.queues["s"]:
                    f(e)

            @block.tensor
            def _(e):
                for f in self.queues["t"]:
                    f(e)

            @block.gpsimd
            def _(e):
                for f in self.queues["g"]:
                    f(e)

            @block.sync
            def _(e):
                for f in self.queues["y"]:
                    f(e)


def ts(i, n=128):
    return slice(i * n, (i + 1) * n)



def build(T=2048, dbg=False, stop_after=None):
    NCH = T // Q
    TH = T + 3
    NTT_R = T // 128
    nc = bass.Bass("TRN2", target_bir_lowering=False, num_devices=NCORES)

    di = {}
    def inp(name, shape):
        di[name] = nc.dram_tensor(name, shape, F32, kind="ExternalInput")
        return di[name]

    x_sh = inp("x_sh", [TH, HID])
    WpT = inp("WpT", [HID, PROJ])
    WoT = inp("WoT", [INTER, HID])
    w1T_sl = inp("w1T_sl", [HID, FFH // NCORES])
    w2T_sl = inp("w2T_sl", [FFH, HID // NCORES])
    convdiag = inp("convdiag", [N_XBC_T * KCONV * 128, 128])
    conv_b = inp("conv_b", [CONVD])
    nw_in = inp("nw_in", [HID])
    gnw_in = inp("gnw_in", [INTER])
    dtb_in = inp("dtb_in", [HEADS])
    alog_in = inp("alog_in", [HEADS])
    dd_in = inp("dd_in", [HEADS])
    ident_in = inp("ident_in", [128, 128])
    tri01_in = inp("tri01_in", [128, 128])
    pexp_in = inp("pexp_in", [HEADS, INTER])
    m2l_in = inp("m2l_in", [9, NCORES])

    out_y = nc.dram_tensor("out_y", [T, HID], F32, kind="ExternalOutput")

    d_xs = nc.dram_tensor("d_xs", [INTER, T], BF16)
    d_gate = nc.dram_tensor("d_gate", [T, INTER], BF16)
    d_ypart = nc.dram_tensor("d_ypart", [T, INTER], BF16)
    d_sloc = nc.dram_tensor("d_sloc", [NCH * 128, INTER], BF16)
    d_srow = nc.dram_tensor("d_srow", [NCH, HEADS * Q], F32)
    d_ynt = nc.dram_tensor("d_ynt", [INTER, T], BF16)
    d_x2 = nc.dram_tensor("d_x2", [T, HID], F32)
    d_x2i = nc.dram_tensor("d_x2i", [HID, T], BF16)
    d_hg = nc.dram_tensor("d_hg", [T, FFH], F32)
    d_hint = nc.dram_tensor("d_hint", [FFH, T], BF16)
    d_tmp = nc.dram_tensor("d_tmp", [1, FFH], F32)

    W1SL = FFH // NCORES
    W2SL = HID // NCORES
    AGW1 = HID * W1SL
    AGW2 = FFH * W2SL
    ag_w1_in = nc.dram_tensor("ag_w1_in", [HID, W1SL], BF16)
    ag_w1_out = nc.dram_tensor("ag_w1_out", [NCORES * HID, W1SL], BF16)
    ag_w2_in = nc.dram_tensor("ag_w2_in", [FFH, W2SL], BF16)
    ag_w2_out = nc.dram_tensor("ag_w2_out", [NCORES * FFH, W2SL], BF16)
    ag_sc_in = nc.dram_tensor("ag_sc_in", [1, W1SL + W2SL], F32)
    ag_sc_out = nc.dram_tensor("ag_sc_out", [NCORES, W1SL + W2SL], F32)
    STN = 128 * INTER + HEADS
    ag_st_in = nc.dram_tensor("ag_st_in", [1, STN], F32)
    ag_st_out = nc.dram_tensor("ag_st_out", [NCORES, STN], F32)

    dbg_t = {}
    if dbg:
        for name, shape in [
            ("dbg_xnt", [128, NKT * TH]), ("dbg_xs", [INTER, T]),
            ("dbg_dt", [HEADS, T]), ("dbg_gate", [T, INTER]),
            ("dbg_ypart", [T, INTER]), ("dbg_srun", [128, INTER]),
            ("dbg_enter", [128, INTER]), ("dbg_y", [T, INTER]),
            ("dbg_x2", [T, HID]), ("dbg_hg", [T, FFH]),
            ("dbg_w1q", [NCORES * HID, W1SL]), ("dbg_sc", [NCORES, W1SL + W2SL]),
            ("dbg_hint", [FFH, T]), ("dbg_w2q", [NCORES * FFH, HID // NCORES]),
            ("dbg_ff", [T, HID]),
            ("dbg_s", [HEADS, T]), ("dbg_M", [128, 128]),
        ]:
            dbg_t[name] = nc.dram_tensor(name, shape, F32, kind="ExternalOutput")

    ctx = contextlib.ExitStack()
    E = ctx.enter_context
    p = Prog(nc, ctx)
    open_scopes = []

    def colsplits(total, step, base=0):
        out = []
        c = base
        while c < total:
            w = min(step, total - c)
            out.append((c, w))
            c += w
        return out

    ident_f = E(nc.sbuf_tensor("ident_f", [128, 128], F32))
    ident_b = E(nc.sbuf_tensor("ident_b", [128, 128], BF16))
    tri01 = E(nc.sbuf_tensor("tri01", [128, 128], F32))
    onescol_b = E(nc.sbuf_tensor("onescol_b", [128, 1], BF16))
    ones1f = E(nc.sbuf_tensor("ones1f", [1, 128], F32))
    cst = E(nc.sbuf_tensor("cst", [128, 8], F32))
    nw_sb = E(nc.sbuf_tensor("nw_sb", [128, NKT], F32))
    cb_sb = E(nc.sbuf_tensor("cb_sb", [128, N_XBC_T], F32))
    dtb_sb = E(nc.sbuf_tensor("dtb_sb", [HEADS, 1], F32))
    negA = E(nc.sbuf_tensor("negA", [HEADS, 1], F32))
    m2l_sb = E(nc.sbuf_tensor("m2l_sb", [9, NCORES], F32))
    tmp64 = E(nc.sbuf_tensor("tmp64", [HEADS, 1], F32))
    ld_sb = E(nc.sbuf_tensor("ld_sb", [HEADS, 1], F32))
    x2scl = E(nc.sbuf_tensor("x2scl", [128, NTT_R], F32))
    x2inv = E(nc.sbuf_tensor("x2inv", [128, NTT_R], F32))
    hscl = E(nc.sbuf_tensor("hscl", [128, NTT_R], F32))
    hinv = E(nc.sbuf_tensor("hinv", [128, NTT_R], F32))
    hmax_run = E(nc.sbuf_tensor("hmax_run", [128, NTT_R], F32))
    ws2col = E(nc.sbuf_tensor("ws2col", [128, HID // 128], F32))

    psA = E(nc.psum_tensor("psA", [128, 2048], F32))
    psB = E(nc.psum_tensor("psB", [128, 1024], F32))
    psT = E(nc.psum_tensor("psT", [128, 1024], BF16))
    psT2 = E(nc.psum_tensor("psT2", [128, 1024], BF16))
    mstack = contextlib.ExitStack()
    open_scopes.append(mstack)
    EM = mstack.enter_context
    gnw_rep = EM(nc.sbuf_tensor("gnw_rep", [128, INTER], BF16))
    drep = EM(nc.sbuf_tensor("drep", [128, HEADS], F32))
    dt_sb = EM(nc.sbuf_tensor("dt_sb", [HEADS, T], F32))
    dtA_sb = EM(nc.sbuf_tensor("dtA_sb", [HEADS, T], F32))
    s_sb = EM(nc.sbuf_tensor("s_sb", [HEADS, T], F32))
    smld_sb = EM(nc.sbuf_tensor("smld_sb", [HEADS, T], F32))
    smldT = EM(nc.sbuf_tensor("smldT", [128, HEADS * NCH], F32))
    sexpT = EM(nc.sbuf_tensor("sexpT", [128, HEADS * NCH], F32))
    dexp = EM(nc.sbuf_tensor("dexp", [128, 32 * NCH], F32))
    B_sb = EM(nc.sbuf_tensor("B_sb", [128, T], BF16))
    C_sb = EM(nc.sbuf_tensor("C_sb", [128, T], BF16))
    onesrow64 = EM(nc.sbuf_tensor("onesrow64", [HEADS, Q], F32))
    PSA01 = ["psA0", "psA1"]


    class _StopBuild(Exception):
        pass

    def _stop(tag):
        if stop_after == tag:
            p.join(skip_cc=False)
            p.emit_all()
            raise _StopBuild()
        return False
    try:
        p.dma("y", "cst", ident_f[:, :], ident_in[:, :], w=["ident_f"])
        p.dma("g", "cstg", ident_b[:, :], ident_in[:, :], w=["ident_b"])
        p.dma("y", "cst", tri01[:, :], tri01_in[:, :], w=["tri01"])
        p.dma("y", "cst", m2l_sb[:, :], m2l_in[:, :], w=["m2l"])
        p.dma("y", "cst", cb_sb[:, :], bass.AP(conv_b, 0, [[1, 128], [128, N_XBC_T]]), w=["cb"], slow=True)
        p.dma("y", "cst", nw_sb[:, :], bass.AP(nw_in, 0, [[1, 128], [128, NKT]]), w=["nw"], slow=True)
        p.dma("y", "cst", dtb_sb[:, :], bass.AP(dtb_in, 0, [[1, HEADS], [1, 1]]), w=["dtb"])
        p.dma("y", "cst", tmp64[:, :], bass.AP(alog_in, 0, [[1, HEADS], [1, 1]]), w=["tmp64"])
        p.op("v", lambda e: e.memset(onescol_b[:, :], 1.0), w=["ones"])
        p.op("v", lambda e: e.memset(ones1f[:, :], 1.0), w=["ones1f"])
        p.op("v", lambda e: e.memset(onesrow64[:, :], 1.0), w=["ones64"])
        p.op("v", lambda e: e.memset(cst[:, 0:1], EPS), w=["cst"])
        p.op("v", lambda e: e.memset(cst[:, 1:2], 87.0), w=["cst87"])
        p.op("v", lambda e: e.memset(cst[:, 2:3], 1.0), w=["cst1"])
        p.op("s", lambda e: e.activation(negA[:, :], tmp64[:, :], AF.Exp), r=["tmp64"], w=["negA"])
        p.op("v", lambda e: e.tensor_scalar_mul(negA[:, :], negA[:, :], -1.0), r=["negA"], w=["negA"])

        def bcast_row(row_ap, width, dest_sb, dest_reg, rregs, scale=None, dt_=None):
            fs = []
            for (c0, w) in colsplits(width, 512):
                fs.append(lambda e, c0=c0, w=w: e.matmul(psA[:, c0 % 2048:c0 % 2048 + w],
                                                         ones1f[:, :], row_ap[0:1, c0:c0 + w],
                                                         start=True, stop=True))
            p.ops("t", fs, r=["ones1f"] + list(rregs), w=PSA01)
            for (c0, w) in colsplits(width, 2048):
                wv = min(2048, width - c0)
                if scale is None:
                    p.op("v", lambda e, c0=c0, wv=wv: e.tensor_copy(dest_sb[:, c0:c0 + wv], psA[:, 0:wv]),
                         r=PSA01, w=[dest_reg])
                else:
                    p.op("v", lambda e, c0=c0, wv=wv: e.tensor_scalar_mul(dest_sb[:, c0:c0 + wv], psA[:, 0:wv], scale),
                         r=PSA01, w=[dest_reg])
            assert width <= 2048

        with contextlib.ExitStack() as p0:
            grow = p0.enter_context(nc.sbuf_tensor("grow", [1, INTER], F32))
            p.dma("y", "cst", grow[0:1, 0:HEADS], bass.AP(dd_in, 0, [[HEADS, 1], [1, HEADS]]), w=["grow"])
            bcast_row(grow, HEADS, drep, "drep", ["grow"])
            p.dma("y", "cst", grow[0:1, 0:INTER], bass.AP(gnw_in, 0, [[INTER, 1], [1, INTER]]),
                  r=["grow"], w=["grow"])
            for (c0, w) in colsplits(INTER, 2048):
                fs = []
                for (s0, sw) in colsplits(w, 512):
                    fs.append(lambda e, c0=c0, s0=s0, sw=sw: e.matmul(psA[:, s0:s0 + sw], ones1f[:, :],
                                                                     grow[0:1, c0 + s0:c0 + s0 + sw],
                                                                     start=True, stop=True))
                p.ops("t", fs, r=["ones1f", "grow"], w=PSA01)
                p.op("v", lambda e, c0=c0, w=w: e.tensor_copy(gnw_rep[:, c0:c0 + w], psA[:, 0:w]),
                     r=PSA01, w=["gnw_rep"])

        p.join()
        with contextlib.ExitStack() as p1:
            E1 = p1.enter_context
            wqf = E1(nc.sbuf_tensor("wqf", [128, NKT * 1024], F32))
            absb = E1(nc.sbuf_tensor("absb", [128, 1024], BF16))
            s1row = E1(nc.sbuf_tensor("s1row", [1, W1SL], F32))
            s2row = E1(nc.sbuf_tensor("s2row", [1, W2SL], F32))
            shrep = E1(nc.sbuf_tensor("shrep", [128, 1024], F32))
            nshrep = E1(nc.sbuf_tensor("nshrep", [128, 1024], F32))
            c1b = E1(nc.sbuf_tensor("c1b", [128, 1024], BF16))
            c2b = E1(nc.sbuf_tensor("c2b", [128, 1024], BF16))
            qb = E1(nc.sbuf_tensor("qb", [128, 1024], BF16))

            def quant_weight(wsl_dram, nkt_w, WSL, agin, srow, tag):
                for k in range(nkt_w):
                    p.dma("y", "w1s", wqf[:, k * WSL:(k + 1) * WSL], wsl_dram[ts(k), :], w=[f"{tag}wqf{k}"])
                for k in range(nkt_w):
                    p.op("s", lambda e, k=k, WSL=WSL: e.activation(absb[:, 0:WSL], wqf[:, k * WSL:(k + 1) * WSL], AF.Abs),
                         r=[f"{tag}wqf{k}"], w=["absb"])
                    fs = []
                    for (c0, w) in colsplits(WSL, 512):
                        fs.append(lambda e, c0=c0, w=w, k=k: e.matmul(psB[0:1, c0:c0 + w], onescol_b[:, :],
                                                                      absb[:, c0:c0 + w],
                                                                      start=(k == 0), stop=(k == nkt_w - 1)))
                    p.ops("t", fs, r=["absb", "ones"], w=["psB"])
                p.op("v", lambda e, WSL=WSL, nkt_w=nkt_w, srow=srow: e.tensor_scalar(
                    srow[0:1, 0:WSL], psB[0:1, 0:WSL], 1.0 / (nkt_w * 128.0), 1e-5, ALU.mult, ALU.max),
                    r=["psB"], w=[f"{tag}srow"])
                fs = []
                for (c0, w) in colsplits(WSL, 512):
                    fs.append(lambda e, c0=c0, w=w, srow=srow: e.matmul(psA[:, c0:c0 + w], ones1f[:, :],
                                                                       srow[0:1, c0:c0 + w], start=True, stop=True))
                p.ops("t", fs, r=["ones1f", f"{tag}srow"], w=PSA01)
                p.op("v", lambda e, WSL=WSL: e.tensor_scalar_mul(shrep[:, 0:WSL], psA[:, 0:WSL], 0.5),
                     r=PSA01, w=["shrep"])
                p.op("v", lambda e, WSL=WSL: e.tensor_scalar_mul(nshrep[:, 0:WSL], psA[:, 0:WSL], -0.5),
                     r=PSA01, w=["nshrep"])
                for k in range(nkt_w):
                    wt = wqf[:, k * WSL:(k + 1) * WSL]
                    p.op("v", lambda e, wt=wt, WSL=WSL: e.tensor_tensor(c1b[:, 0:WSL], wt, shrep[:, 0:WSL], ALU.is_gt),
                         r=[f"{tag}wqf{k}", "shrep"], w=["c1b"])
                    p.op("v", lambda e, wt=wt, WSL=WSL: e.tensor_tensor(c2b[:, 0:WSL], wt, nshrep[:, 0:WSL], ALU.is_lt),
                         r=[f"{tag}wqf{k}", "nshrep"], w=["c2b"])
                    p.op("v", lambda e, WSL=WSL: e.tensor_tensor(qb[:, 0:WSL], c1b[:, 0:WSL], c2b[:, 0:WSL], ALU.subtract),
                         r=["c1b", "c2b"], w=["qb"])
                    p.dma("s", "wqo", agin[ts(k), :], qb[:, 0:WSL], r=["qb"], w=[f"{tag}ag{k}"])

            quant_weight(w1T_sl, NKT, W1SL, ag_w1_in, s1row, "w1")
            quant_weight(w2T_sl, FFH // 128, W2SL, ag_w2_in, s2row, "w2")
            p.dma("s", "wqo", ag_sc_in[0:1, 0:W1SL], s1row[0:1, :], r=["w1srow"], w=["agsc1"])
            p.dma("s", "wqo", ag_sc_in[0:1, W1SL:W1SL + W2SL], s2row[0:1, :], r=["w2srow"], w=["agsc2"])
            p.collective(bass.AP(ag_w1_in, 0, [[1, 1], [1, AGW1]]), bass.AP(ag_w1_out, 0, [[1, 1], [1, NCORES * AGW1]]),
                         r=[f"w1ag{k}" for k in range(NKT)], w=["AGW1"])
            p.collective(bass.AP(ag_w2_in, 0, [[1, 1], [1, AGW2]]), bass.AP(ag_w2_out, 0, [[1, 1], [1, NCORES * AGW2]]),
                         r=[f"w2ag{k}" for k in range(FFH // 128)], w=["AGW2"])
            p.collective(bass.AP(ag_sc_in, 0, [[1, 1], [1, W1SL + W2SL]]),
                         bass.AP(ag_sc_out, 0, [[1, 1], [1, NCORES * (W1SL + W2SL)]]),
                         r=["agsc1", "agsc2"], w=["AGSC"])

        _stop('P1')
        p.join()
        with contextlib.ExitStack() as P2:
            E2o = P2.enter_context
            x_nT = E2o(nc.sbuf_tensor("x_nT", [128, NKT * TH], BF16))
            nwrep = E2o(nc.sbuf_tensor("nwrep", [128, HID], F32))
            growp = E2o(nc.sbuf_tensor("growp", [1, HID], F32))
            p.dma("y", "cst", growp[0:1, 0:HID], bass.AP(nw_in, 0, [[HID, 1], [1, HID]]), w=["growp"])
            bcast_row(growp, HID, nwrep, "nwrep", ["growp"])
            NTT = (TH + 127) // 128
            with contextlib.ExitStack() as p2a:
                E2 = p2a.enter_context
                xld = [E2(nc.sbuf_tensor(f"xld{i}", [128, HID], F32)) for i in range(2)]
                xsq = E2(nc.sbuf_tensor("xsq", [128, HID], F32))
                sscol = E2(nc.sbuf_tensor("sscol", [128, 1], F32))
                rstd = E2(nc.sbuf_tensor("rstd", [128, 1], F32))
                xnb = E2(nc.sbuf_tensor("xnb", [128, HID], BF16))
                for r in range(NTT):
                    nrow = min(128, TH - r * 128)
                    bi = r % 2
                    p.dma("y", f"xl{bi}", xld[bi][0:nrow, :], x_sh[r * 128:r * 128 + nrow, :], w=[f"xld{bi}"])
                    p.op("s", lambda e, b=bi, n=nrow: e.activation(xsq[0:n, :], xld[b][0:n, :], AF.Square,
                                                                   accum_out=sscol[0:n, :]),
                         r=[f"xld{bi}"], w=["xsq", "sscol"])
                    p.op("s", lambda e, n=nrow: e.activation(rstd[0:n, :], sscol[0:n, :], AF.Ln,
                                                             bias=cst[0:n, 0:1], scale=1.0 / HID),
                         r=["sscol", "cst"], w=["rstd"])
                    p.op("s", lambda e, n=nrow: e.activation(rstd[0:n, :], rstd[0:n, :], AF.Exp, scale=-0.5),
                         r=["rstd"], w=["rstd"])
                    p.op("v", lambda e, b=bi, n=nrow: e.scalar_tensor_tensor(xnb[0:n, :], xld[b][0:n, :], rstd[0:n, 0:1],
                                                                             nwrep[0:n, :], ALU.mult, ALU.mult),
                         r=[f"xld{bi}", "rstd", "nwrep"], w=["xnb"])
                    for half in range(2):
                        p.ops("t", [(lambda e, j=j, h=half, n=nrow:
                                     e.transpose(psT[:, j * 128:j * 128 + n], xnb[0:n, ts(h * 8 + j)], ident_b[0:n, 0:n]))
                                    for j in range(8)],
                              r=["xnb", "ident_b"], w=["psT"])
                        p.op("v", lambda e, h=half, r0=r, n=nrow: e.tensor_copy(
                                 bass.AP(x_nT, (h * 8) * TH + r0 * 128, [[NKT * TH, 128], [TH, 8], [1, n]]),
                                 bass.AP(psT, 0, [[1024, 128], [128, 8], [1, n]])),
                             r=["psT"], w=[f"xnT_{r}_{half}"])
            XNT_ALL = [f"xnT_{r}_{h}" for r in range(NTT) for h in range(2)]
            if dbg:
                p.join()
                for k in range(NKT):
                    p.dma("g", "dbgo2", dbg_t["dbg_xnt"][:, k * TH:(k + 1) * TH],
                          x_nT[:, k * TH:(k + 1) * TH], r=XNT_ALL, w=["dbg1"])

            p.join()
            PSA_W = min(TH, 2048)
            with contextlib.ExitStack() as p2b:
                E2 = p2b.enter_context
                wp_b = [E2(nc.sbuf_tensor(f"wp_b{i}", [128, NKT * 512], BF16)) for i in range(2)]
                cdg_b = [E2(nc.sbuf_tensor(f"cdg_b{i}", [128, 16 * 128], BF16)) for i in range(2)]
                xbc_sb = E2(nc.sbuf_tensor("xbc_sb", [128, TH], BF16))
                xs_out = E2(nc.sbuf_tensor("xs_out", [128, T], BF16))

                def load_wchunk(ci, c0, w):
                    p.dma("g", f"wpc{ci % 2}", bass.AP(wp_b[ci % 2], 0, [[NKT * 512, 128], [512, NKT], [1, w]]),
                          bass.AP(WpT, c0, [[PROJ, 128], [128 * PROJ, NKT], [1, w]]),
                          w=[f"wpb{ci % 2}"])

                def load_cdg(ci, ctl0, ntile):
                    p.dma("g", f"cdg{ci % 2}", bass.AP(cdg_b[ci % 2], 0, [[16 * 128, 128], [128, ntile * KCONV], [1, 128]]),
                          bass.AP(convdiag, ctl0 * KCONV * 128 * 128, [[128, 128], [128 * 128, ntile * KCONV], [1, 128]]),
                          w=[f"cdgb{ci % 2}"])

                chunks = [(INTER + 512 * i, 512) for i in range(8)] + [(INTER + 4096, 320)]
                load_wchunk(0, chunks[0][0], chunks[0][1])
                load_cdg(0, 0, 4)
                for ci, (c0, wdt) in enumerate(chunks):
                    if ci + 1 < len(chunks):
                        load_wchunk(ci + 1, chunks[ci + 1][0], chunks[ci + 1][1])
                        load_cdg(ci + 1, (ci + 1) * 4, 4 if ci + 1 < 8 else 2)
                    wreg = f"wpb{ci % 2}"
                    creg = f"cdgb{ci % 2}"
                    for tl in range((wdt + 127) // 128):
                        is_dt = (ci == 8 and tl == 2)
                        m = 64 if is_dt else 128
                        wb = wp_b[ci % 2]
                        fs = []
                        for k in range(NKT):
                            for (n0, nw) in colsplits(PSA_W, 512):
                                fs.append(lambda e, k=k, n0=n0, nw=nw, m=m, tl=tl, wb=wb: e.matmul(
                                    psA[0:m, n0:n0 + nw], wb[:, k * 512 + tl * 128: k * 512 + tl * 128 + m],
                                    x_nT[:, k * TH + n0: k * TH + n0 + nw],
                                    start=(k == 0), stop=(k == NKT - 1)))
                            if TH > 2048:
                                fs.append(lambda e, k=k, m=m, tl=tl, wb=wb: e.matmul(
                                    psB[0:m, 0:TH - 2048],
                                    wb[:, k * 512 + tl * 128: k * 512 + tl * 128 + m],
                                    x_nT[:, k * TH + 2048: k * TH + TH],
                                    start=(k == 0), stop=(k == NKT - 1)))
                        p.ops("t", fs, r=[wreg] + XNT_ALL, w=PSA01 + ["psB"])
                        ctl = ci * 4 + tl
                        if is_dt:
                            na = min(T, 2048 - 3)
                            p.op("s", lambda e, na=na: e.activation(dt_sb[:, 0:na], psA[0:HEADS, 3:3 + na], AF.Identity,
                                                                    bias=dtb_sb[:, :]),
                                 r=PSA01 + ["dtb"], w=["dt_a"])
                            if TH > 2048:
                                p.op("s", lambda e, na=na: e.activation(dt_sb[:, na:T], psB[0:HEADS, 0:T - na],
                                                                        AF.Identity, bias=dtb_sb[:, :]),
                                     r=["psB", "dtb"], w=["dt_b"])
                            else:
                                p.op("v", lambda e: e.engine_nop(), w=["dt_b"])
                        else:
                            p.op("v", lambda e: e.tensor_copy(xbc_sb[:, 0:PSA_W], psA[:, 0:PSA_W]),
                                 r=PSA01, w=["xbc0"])
                            if TH > 2048:
                                p.op("v", lambda e: e.tensor_copy(xbc_sb[:, 2048:TH], psB[:, 0:TH - 2048]),
                                     r=["psB"], w=["xbc1"])
                            else:
                                p.op("v", lambda e: e.engine_nop(), w=["xbc1"])
                            cbase = tl * KCONV * 128
                            cb_ = cdg_b[ci % 2]
                            rounds = colsplits(T, 1024)
                            for ri, (r0, rw) in enumerate(rounds):
                                fs = []
                                for (s0, sw) in colsplits(rw, 512):
                                    for tap in range(KCONV):
                                        fs.append(lambda e, r0=r0, s0=s0, sw=sw, tap=tap, cbase=cbase, cb_=cb_: e.matmul(
                                            psB[:, s0:s0 + sw],
                                            cb_[:, cbase + tap * 128: cbase + (tap + 1) * 128],
                                            xbc_sb[:, r0 + s0 + tap: r0 + s0 + tap + sw],
                                            start=(tap == 0), stop=(tap == KCONV - 1)))
                                p.ops("t", fs, r=["xbc0", "xbc1", creg], w=["psB"])
                                p.op("s", lambda e, r0=r0, rw=rw, ctl=ctl: e.activation(
                                    xs_out[:, r0:r0 + rw], psB[:, 0:rw], AF.Silu, bias=cb_sb[:, ctl:ctl + 1]),
                                    r=["psB", "cb"], w=[f"xsout{ri}"])
                            xsregs = [f"xsout{r2}" for r2 in range(len(rounds))]
                            if ctl < N_XS_T:
                                p.dma("s", "xsp", d_xs[ts(ctl), :], xs_out[:, :], r=xsregs, w=[f"dxs{ctl}"])
                            elif ctl == N_XS_T:
                                p.op("v", lambda e: e.tensor_copy(B_sb[:, :], xs_out[:, :]), r=xsregs, w=["B_sb"])
                            else:
                                p.op("v", lambda e: e.tensor_copy(C_sb[:, :], xs_out[:, :]), r=xsregs, w=["C_sb"])
            if dbg:
                p.join()
                for ctl in range(N_XS_T):
                    p.dma("g", "dbgo2", dbg_t["dbg_xs"][ts(ctl), :], d_xs[ts(ctl), :], r=[f"dxs{ctl}"], w=["dbg2"])
                p.dma("s", "dbgo", dbg_t["dbg_dt"][:, :], dt_sb[0:HEADS, :], r=["dt_a", "dt_b"], w=["dbg3"])

            p.join()
            with contextlib.ExitStack() as p2c:
                E2 = p2c.enter_context
                wg_b = [E2(nc.sbuf_tensor(f"wg_b{i}", [128, NKT * 1024], BF16)) for i in range(2)]
                gsil = E2(nc.sbuf_tensor("gsil", [128, 1024], BF16))

                def load_gchunk(gi):
                    p.dma("g", f"wgc{gi % 2}", bass.AP(wg_b[gi % 2], 0, [[NKT * 1024, 128], [1024, NKT], [1, 1024]]),
                          bass.AP(WpT, gi * 1024, [[PROJ, 128], [128 * PROJ, NKT], [1, 1024]]),
                          w=[f"wgb{gi % 2}"])

                load_gchunk(0)
                for gi in range(INTER // 1024):
                    if gi + 1 < INTER // 1024:
                        load_gchunk(gi + 1)
                    wreg = f"wgb{gi % 2}"
                    wgb = wg_b[gi % 2]
                    for tt in range(NTT_R):
                        bank = (tt % 2) * 1024
                        fs = []
                        for k in range(NKT):
                            for half in range(2):
                                fs.append(lambda e, k=k, half=half, tt=tt, wgb=wgb, bank=bank: e.matmul(
                                    psA[:, bank + half * 512: bank + (half + 1) * 512],
                                    x_nT[:, k * TH + 3 + tt * 128: k * TH + 3 + (tt + 1) * 128],
                                    wgb[:, k * 1024 + half * 512: k * 1024 + (half + 1) * 512],
                                    start=(k == 0), stop=(k == NKT - 1)))
                        p.ops("t", fs, r=[wreg] + XNT_ALL, w=[f"psA{tt % 2}"])
                        p.op("s", lambda e, bank=bank: e.activation(gsil[:, :], psA[:, bank:bank + 1024], AF.Silu),
                             r=[f"psA{tt % 2}"], w=["gsil"])
                        p.dma("s", "gsp", d_gate[ts(tt), gi * 1024:(gi + 1) * 1024], gsil[:, :],
                              r=["gsil"], w=[f"dgate{gi}_{tt}"])
            GATE_ALL = [f"dgate{gi}_{tt}" for gi in range(INTER // 1024) for tt in range(NTT_R)]
            if dbg:
                p.join()
                for tt in range(NTT_R):
                    p.dma("g", "dbgo2", dbg_t["dbg_gate"][ts(tt), :], d_gate[ts(tt), :], r=GATE_ALL, w=["dbg4"])

        _stop('P2')
        p.join()
        DT_ALL = ["dt_a", "dt_b"]
        p.op("s", lambda e: e.activation(smld_sb[:, :], dt_sb[:, :], AF.Abs),
             r=DT_ALL, w=["smld"])
        p.op("s", lambda e: e.activation(dtA_sb[:, :], smld_sb[:, :], AF.Exp, scale=-1.0),
             r=["smld"], w=["dtA"])
        p.op("s", lambda e: e.activation(smld_sb[:, :], dtA_sb[:, :], AF.Ln, bias=cst[0:HEADS, 2:3]),
             r=["dtA", "cst1"], w=["smld"])
        p.op("v", lambda e: e.scalar_tensor_tensor(dt_sb[:, :], dt_sb[:, :], 0.0, smld_sb[:, :],
                                                   ALU.max, ALU.add),
             r=DT_ALL + ["smld"], w=["dt_a", "dt_b"])
        p.op("v", lambda e: e.tensor_scalar_mul(dtA_sb[:, :], dt_sb[:, :], negA[:, 0:1]),
             r=DT_ALL + ["negA", "dtA"], w=["dtA"])
        p.op("v", lambda e: e.reduce_sum(ld_sb[:, :], dtA_sb[:, :], axis=mybir.AxisListType.X),
             r=["dtA"], w=["ld"])
        _stop('P3s')
        for kc in range(NCH):
            p.op("v", lambda e, kc=kc: e.tensor_tensor_scan(s_sb[:, kc * Q:(kc + 1) * Q], onesrow64[:, :],
                                                            dtA_sb[:, kc * Q:(kc + 1) * Q], 0.0, ALU.mult, ALU.add),
                 r=["dtA", "ones64"], w=[f"s{kc}"])
        S_ALL = [f"s{kc}" for kc in range(NCH)]
        _stop('P3t')
        p.op("s", lambda e: e.activation(dtA_sb[:, :], dt_sb[:, :], AF.Ln), r=DT_ALL + ["dtA", "ld"], w=["dtA"])
        p.op("v", lambda e: e.tensor_tensor(smld_sb[:, :], s_sb[:, :], dtA_sb[:, :], ALU.subtract),
             r=S_ALL + ["dtA"], w=["smld"])
        _stop('P3u')
        for kc in range(NCH):
            p.ops("t", [
                lambda e, kc=kc: e.transpose(psB[:, 0:64], s_sb[0:HEADS, kc * Q:(kc + 1) * Q], ident_f[0:HEADS, 0:HEADS]),
                lambda e, kc=kc: e.transpose(psB[:, 64:128], smld_sb[0:HEADS, kc * Q:(kc + 1) * Q],
                                             ident_f[0:HEADS, 0:HEADS]),
            ], r=S_ALL + ["smld", "ident_f"], w=["psB"])
            p.op("s", lambda e, kc=kc: e.activation(sexpT[:, kc * HEADS:(kc + 1) * HEADS], psB[:, 0:64], AF.Exp),
                 r=["psB"], w=[f"sexpT{kc}"])
            p.op("v", lambda e, kc=kc: e.tensor_copy(smldT[:, kc * HEADS:(kc + 1) * HEADS], psB[:, 64:128]),
                 r=["psB"], w=[f"smldT{kc}"])
        for kc in range(NCH):
            p.dma("y", "srsp", bass.AP(d_srow, kc * HEADS * Q, [[Q, HEADS], [1, Q]]),
                  s_sb[0:HEADS, kc * Q:(kc + 1) * Q], r=S_ALL, w=[f"dsrow{kc}"])
        if dbg:
            p.dma("s", "dbgo", dbg_t["dbg_s"][:, :], s_sb[0:HEADS, :], r=S_ALL, w=["dbgs"])
        _stop('P3p')

        with contextlib.ExitStack() as P34:
            E3o = P34.enter_context
            S_runT = E3o(nc.sbuf_tensor("S_runT", [128, INTER], F32))
            enterT_state = E3o(nc.sbuf_tensor("enterT_state", [128, INTER], F32))
            dche = E3o(nc.sbuf_tensor("dche", [HEADS, NCH], F32))

            with contextlib.ExitStack() as p3p:
                pexp2 = p3p.enter_context(nc.sbuf_tensor("pexp2", [HEADS, INTER], F32))
                p.dma("y", "cst", pexp2[:, :], pexp_in[:, :], w=["pexp2"])
                p.op("v", lambda e: e.tensor_copy(dche[:, :], bass.AP(s_sb, Q - 1, [[T, HEADS], [Q, NCH]])),
                     r=S_ALL, w=["dche"])
                p.op("s", lambda e: e.activation(dche[:, :], dche[:, :], AF.Exp), r=["dche"], w=["dche"])
                fs = []
                for t in range(32):
                    fs.append(lambda e, t=t: e.matmul(psA[:, t * NCH:(t + 1) * NCH], pexp2[:, t * 128:(t + 1) * 128],
                                                      dche[:, :], start=True, stop=True))
                p.ops("t", fs, r=["pexp2", "dche"], w=["psA0"])
                p.op("v", lambda e: e.tensor_copy(dexp[:, 0:32 * NCH], psA[:, 0:32 * NCH]), r=["psA0"], w=["dexp"])

            p.join()
            _stop('P3a')
            with contextlib.ExitStack() as p3:
                E3 = p3.enter_context
                srow_b = [E3(nc.sbuf_tensor("srow0", [1, HEADS * Q], F32))]
                xsch = [E3(nc.sbuf_tensor(f"xsch{i}", [128, INTER], BF16)) for i in range(2)]
                xsT_ch = [E3(nc.sbuf_tensor(f"xsT{i}", [128, INTER], BF16)) for i in range(2)]
                Ech = [E3(nc.sbuf_tensor(f"Ech{i}", [128, 1024], F32)) for i in range(2)]
                Eexp = [E3(nc.sbuf_tensor(f"Eexp{i}", [128, 1024], F32)) for i in range(2)]
                Pm_sb = E3(nc.sbuf_tensor("Pm_sb", [128, 128], F32))
                BT_sb = E3(nc.sbuf_tensor("BT_sb", [128, 128], BF16))
                M_sb = [E3(nc.sbuf_tensor(f"M_sb{i}", [128, 1024], BF16)) for i in range(2)]
                tmpv = E3(nc.sbuf_tensor("tmpv", [128, 512], F32))
                ypartb = E3(nc.sbuf_tensor("ypartb", [128, INTER], BF16))
                xsw_sb = E3(nc.sbuf_tensor("xsw_sb", [128, 512], BF16))
                slocf = E3(nc.sbuf_tensor("slocf", [128, 512], F32))

                def load_xs(kc):
                    p.dma("y", f"xsr{kc % 2}", bass.AP(xsch[kc % 2], 0, [[INTER, 128], [128, 32], [1, 128]]),
                          bass.AP(d_xs, kc * 128, [[T, 128], [128 * T, 32], [1, 128]]),
                          r=[f"dxs{ctl}" for ctl in range(N_XS_T)], w=[f"xsch{kc % 2}"])

                def load_srow(kc):
                    p.dma("y", "srr0", srow_b[0][0:1, :],
                          bass.AP(d_srow, kc * HEADS * Q, [[1, 1], [1, HEADS * Q]]),
                          r=[f"dsrow{kc}"], w=["srowb0"])

                load_xs(0)
                for kc in range(NCH):
                    load_srow(kc)
                    if kc + 1 < NCH:
                        load_xs(kc + 1)
                    xr = f"xsch{kc % 2}"
                    xb = xsch[kc % 2]
                    xT = xsT_ch[kc % 2]
                    xTreg = f"xsT{kc % 2}"
                    for g in range(4):
                        p.ops("t", [(lambda e, j=j, g=g, xb=xb: e.transpose(psT[:, j * 128:(j + 1) * 128],
                                                                           xb[:, (g * 8 + j) * 128:(g * 8 + j + 1) * 128],
                                                                           ident_b[:, :]))
                                    for j in range(8)], r=[xr, "ident_b"], w=["psT"])
                        p.op("v", lambda e, g=g, xT=xT: e.tensor_copy(xT[:, g * 1024:(g + 1) * 1024], psT[:, :]),
                             r=["psT"], w=[xTreg])
                    p.ops("t", [lambda e, kc=kc: e.matmul(psB[:, 0:128], B_sb[:, kc * Q:(kc + 1) * Q],
                                                          C_sb[:, kc * Q:(kc + 1) * Q], start=True, stop=True)],
                          r=["B_sb", "C_sb"], w=["psB0"])
                    p.op("v", lambda e: e.tensor_tensor(Pm_sb[:, :], psB[:, 0:128], tri01[:, :], ALU.mult),
                         r=["psB0", "tri01"], w=["Pm"])
                    p.ops("t", [lambda e, kc=kc: e.transpose(psT2[:, 0:128], B_sb[:, kc * Q:(kc + 1) * Q], ident_b[:, :])],
                          r=["B_sb", "ident_b"], w=["psT2"])
                    p.op("v", lambda e: e.tensor_copy(BT_sb[:, :], psT2[:, 0:128]), r=["psT2"], w=["BT"])
                    srw = srow_b[0]
                    srwreg = "srowb0"

                    def emit_bc(hg):
                        pa = hg % 2
                        bb = pa * 1024
                        fs = []
                        for half in range(2):
                            fs.append(lambda e, hg=hg, half=half, bb=bb: e.matmul(
                                psA[:, bb + half * 512:bb + (half + 1) * 512],
                                ones1f[:, :], srw[0:1, hg * 1024 + half * 512:hg * 1024 + (half + 1) * 512],
                                start=True, stop=True))
                        p.ops("t", fs, r=["ones1f", srwreg], w=[f"psA{pa}"])

                    emit_bc(0)
                    for hg in range(8):
                        pa = hg % 2
                        bb = pa * 1024
                        Ech_ = Ech[pa]
                        Eexp_ = Eexp[pa]
                        M_ = M_sb[pa]
                        fs = []
                        for hh in range(8):
                            h = hg * 8 + hh
                            slot = bb + hh * 128
                            fs.append(lambda e, h=h, hh=hh, slot=slot, kc=kc, Ech_=Ech_: e.scalar_tensor_tensor(
                                Ech_[:, hh * 128:(hh + 1) * 128], psA[:, slot:slot + 128],
                                smldT[:, kc * HEADS + h:kc * HEADS + h + 1],
                                bass.AP(cst, 1, [[8, 128], [0, 128]]), ALU.subtract, ALU.min))
                        p.ops("v", fs, r=[f"psA{pa}", f"smldT{kc}", "cst87"], w=[f"Ech{pa}"])
                        if hg + 1 < 8:
                            emit_bc(hg + 1)
                        p.op("s", lambda e, Ech_=Ech_, Eexp_=Eexp_: e.activation(Eexp_[:, :], Ech_[:, :], AF.Exp),
                             r=[f"Ech{pa}"], w=[f"Eexp{pa}"])
                        p.ops("v", [(lambda e, hh=hh, Eexp_=Eexp_, M_=M_: e.tensor_tensor(
                            M_[:, hh * 128:(hh + 1) * 128],
                            Eexp_[:, hh * 128:(hh + 1) * 128], Pm_sb[:, :], ALU.mult)) for hh in range(4)],
                              r=[f"Eexp{pa}", "Pm"], w=[f"Ma{pa}"])
                        p.ops("g", [(lambda e, hh=hh, Eexp_=Eexp_, M_=M_: e.tensor_tensor(
                            M_[:, hh * 128:(hh + 1) * 128],
                            Eexp_[:, hh * 128:(hh + 1) * 128], Pm_sb[:, :], ALU.mult)) for hh in range(4, 8)],
                              r=[f"Eexp{pa}", "Pm"], w=[f"Mb{pa}"])
                        fs = []
                        for hh in range(8):
                            h = hg * 8 + hh
                            fs.append(lambda e, h=h, hh=hh, xT=xT, M_=M_: e.matmul(
                                psB[:, hh * 64:(hh + 1) * 64], M_[:, hh * 128:(hh + 1) * 128],
                                xT[:, h * 64:(h + 1) * 64], start=True, stop=True))
                        p.ops("t", fs, r=[f"Ma{pa}", f"Mb{pa}", xTreg], w=["psB0"])
                        p.op("v", lambda e, hg=hg, xT=xT: e.tensor_tensor(
                            tmpv[:, :], xT[:, hg * 512:(hg + 1) * 512],
                            bass.AP(drep, hg * 8, [[HEADS, 128], [1, 8], [0, HEADDIM]]), ALU.mult),
                            r=[xTreg, "drep"], w=["tmpv"])
                        p.op("v", lambda e, hg=hg: e.tensor_tensor(ypartb[:, hg * 512:(hg + 1) * 512], tmpv[:, :],
                                                                   psB[:, 0:512], ALU.add),
                             r=["tmpv", "psB0"], w=[f"ypb{hg}"])
                        p.op("g", lambda e, hg=hg, xT=xT, Eexp_=Eexp_: e.tensor_tensor(
                            xsw_sb[:, :], xT[:, hg * 512:(hg + 1) * 512],
                            bass.AP(Eexp_, Q - 1, [[1024, 128], [128, 8], [0, HEADDIM]]), ALU.mult),
                            r=[xTreg, f"Eexp{pa}"], w=["xsw"])
                        fs = []
                        for pr in range(4):
                            fs.append(lambda e, pr=pr: e.matmul(
                                psB[0:128, 512 + pr * 128:512 + (pr + 1) * 128],
                                xsw_sb[:, pr * 128:(pr + 1) * 128], BT_sb[:, :],
                                start=True, stop=True))
                        p.ops("t", fs, r=["xsw", "BT"], w=["psB1"])
                        p.op("s", lambda e: e.activation(slocf[:, :], psB[:, 512:1024], AF.Identity),
                             r=["psB1"], w=["slocf"])
                        p.dma("g", "slsp", d_sloc[kc * 128:(kc + 1) * 128, hg * 512:(hg + 1) * 512], slocf[:, :],
                              r=["slocf"], w=[f"dsl{kc}_{hg}"])
                        if kc > 0:
                            p.op("g", lambda e, hg=hg, kc=kc: e.tensor_tensor(
                                S_runT[:, hg * 512:(hg + 1) * 512], S_runT[:, hg * 512:(hg + 1) * 512],
                                bass.AP(dexp, (hg * 4) * NCH + kc, [[32 * NCH, 128], [NCH, 4], [0, 128]]),
                                ALU.mult),
                                 r=["dexp", "S_run"], w=["S_run"])
                            p.op("g", lambda e, hg=hg: e.tensor_tensor(
                                S_runT[:, hg * 512:(hg + 1) * 512], S_runT[:, hg * 512:(hg + 1) * 512],
                                slocf[:, :], ALU.add),
                                 r=["slocf", "S_run"], w=["S_run"])
                        else:
                            p.op("g", lambda e, hg=hg: e.tensor_copy(
                                S_runT[:, hg * 512:(hg + 1) * 512], slocf[:, :]),
                                 r=["slocf"], w=["S_run"])
                    p.dma("s", "ypsp", d_ypart[kc * 128:(kc + 1) * 128, :], ypartb[:, :],
                          r=[f"ypb{g}" for g in range(8)], w=[f"dyp{kc}"])
                    if kc == 0:
                        _stop('P3c0')
            p.join()
            _stop('P3')
            p.dma("s", "stp", bass.AP(ag_st_in, 0, [[STN, 1], [1, 128 * INTER]]),
                  bass.AP(S_runT, 0, [[INTER, 128], [1, INTER]]), r=["S_run"], w=["agst_a"])
            p.dma("s", "stp", ag_st_in[0:1, 128 * INTER:STN], ld_sb[0:HEADS, 0:1], r=["ld"], w=["agst_b"])
            p.collective(bass.AP(ag_st_in, 0, [[1, 1], [1, STN]]),
                         bass.AP(ag_st_out, 0, [[1, 1], [1, NCORES * STN]]),
                         r=["agst_a", "agst_b"], w=["AGST"])
            if dbg:
                p.dma("s", "dbgo", dbg_t["dbg_srun"][:, :], S_runT[:, :], r=["S_run"], w=["dbgsr"])
                for kc in range(NCH):
                    p.dma("g", "dbgo2", dbg_t["dbg_ypart"][kc * 128:(kc + 1) * 128, :],
                          d_ypart[kc * 128:(kc + 1) * 128, :], r=[f"dyp{kc}"], w=["dbgyp"])

            p.join()
            with contextlib.ExitStack() as p4:
                E4 = p4.enter_context
                pexp3 = E4(nc.sbuf_tensor("pexp3", [HEADS, INTER], F32))
                ldall = E4(nc.sbuf_tensor("ldall", [9, HEADS], F32))
                lwa = E4(nc.sbuf_tensor("lwa", [NCORES, HEADS], F32))
                lwaT = E4(nc.sbuf_tensor("lwaT", [HEADS, NCORES], F32))
                cexp = E4(nc.sbuf_tensor("cexp", [128, 32 * NCORES], F32))
                srbuf = [E4(nc.sbuf_tensor(f"srbuf{i}", [128, INTER], F32)) for i in range(2)]
                p.dma("y", "cst", pexp3[:, :], pexp_in[:, :], w=["pexp3"])
                p.op("v", lambda e: e.memset(ldall[:, :], 1.0), w=["ldall1"])
                p.dma("y", "ldl", ldall[0:NCORES, :], bass.AP(ag_st_out, 128 * INTER, [[STN, NCORES], [1, HEADS]]),
                      r=["AGST", "ldall1"], w=["ldall0"])
                p.ops("t", [lambda e: e.matmul(psB[0:NCORES, 0:HEADS], m2l_sb[:, :], ldall[:, :],
                                               start=True, stop=True)],
                      r=["m2l", "ldall0", "ldall1"], w=["psB"])
                p.op("v", lambda e: e.tensor_copy(lwa[:, :], psB[0:NCORES, 0:HEADS]), r=["psB"], w=["lwa"])
                p.ops("t", [lambda e: e.transpose(psB[0:HEADS, 128:128 + NCORES], lwa[:, :], ident_f[0:NCORES, 0:NCORES])],
                      r=["lwa", "ident_f"], w=["psB"])
                p.op("v", lambda e: e.tensor_copy(lwaT[:, :], psB[0:HEADS, 128:128 + NCORES]), r=["psB"], w=["lwaT"])
                fs = []
                for t in range(32):
                    fs.append(lambda e, t=t: e.matmul(psA[:, t * NCORES:(t + 1) * NCORES],
                                                      pexp3[:, t * 128:(t + 1) * 128], lwaT[:, :],
                                                      start=True, stop=True))
                p.ops("t", fs, r=["pexp3", "lwaT"], w=["psA0"])
                p.op("s", lambda e: e.activation(cexp[:, :], psA[:, 0:32 * NCORES], AF.Exp), r=["psA0"], w=["cexp"])

                def load_sr(r):
                    p.dma("y", f"srl{r % 2}", srbuf[r % 2][:, :],
                          bass.AP(ag_st_out, r * STN, [[INTER, 128], [1, INTER]]),
                          r=["AGST"], w=[f"srbuf{r % 2}"])

                load_sr(0)
                for r in range(NCORES):
                    if r + 1 < NCORES:
                        load_sr(r + 1)
                    sb_ = srbuf[r % 2]
                    sreg = f"srbuf{r % 2}"
                    for t4 in range(32):
                        if r == 0:
                            p.op("v", lambda e, t4=t4, sb_=sb_, r=r: e.tensor_scalar_mul(
                                enterT_state[:, t4 * 128:(t4 + 1) * 128], sb_[:, t4 * 128:(t4 + 1) * 128],
                                cexp[:, t4 * NCORES + r:t4 * NCORES + r + 1]),
                                r=[sreg, "cexp"], w=["enter"])
                        else:
                            p.op("v", lambda e, t4=t4, sb_=sb_, r=r: e.scalar_tensor_tensor(
                                enterT_state[:, t4 * 128:(t4 + 1) * 128], sb_[:, t4 * 128:(t4 + 1) * 128],
                                cexp[:, t4 * NCORES + r:t4 * NCORES + r + 1],
                                enterT_state[:, t4 * 128:(t4 + 1) * 128], ALU.mult, ALU.add),
                                r=[sreg, "cexp", "enter"], w=["enter"])
            if dbg:
                p.join()
                p.dma("s", "dbgo", dbg_t["dbg_enter"][:, :], enterT_state[:, :], r=["enter"], w=["dbgen"])

            p.join()
            _stop('P4')
            with contextlib.ExitStack() as p5a:
                E5 = p5a.enter_context
                slocb = [E5(nc.sbuf_tensor(f"slocb{i}", [128, INTER], BF16)) for i in range(3)]
                ypr = [E5(nc.sbuf_tensor(f"ypr{i}", [128, INTER], BF16)) for i in range(2)]
                gtr = [E5(nc.sbuf_tensor(f"gtr{i}", [128, INTER], BF16)) for i in range(2)]
                enterT_b = E5(nc.sbuf_tensor("enterT_b", [128, INTER], BF16))
                entb_stage = E5(nc.sbuf_tensor("entb_stage", [128, INTER // 2], BF16))
                tmpv2 = E5(nc.sbuf_tensor("tmpv2", [128, 512], F32))
                ycha = E5(nc.sbuf_tensor("ycha", [128, INTER], F32))
                ychb = E5(nc.sbuf_tensor("ychb", [128, INTER], F32))
                gss = E5(nc.sbuf_tensor("gss", [128, 1], F32))
                grst = E5(nc.sbuf_tensor("grst", [128, 1], F32))
                ynb = E5(nc.sbuf_tensor("ynb", [128, INTER], BF16))
                ynt_sb = E5(nc.sbuf_tensor("ynt_sb", [128, 1024], BF16))

                def load_sloc(kc):
                    p.dma("g", f"slr{kc % 3}", slocb[kc % 3][:, :], d_sloc[kc * 128:(kc + 1) * 128, :],
                          r=[f"dsl{kc}_{hg}" for hg in range(8)], w=[f"slocb{kc % 3}"])

                def load_yp(kc):
                    p.dma("y", f"ypr{kc % 2}", ypr[kc % 2][:, :], d_ypart[kc * 128:(kc + 1) * 128, :],
                          r=[f"dyp{kc}"], w=[f"ypr{kc % 2}"])

                def load_gt(kc):
                    p.dma("y", f"gtr{kc % 2}", gtr[kc % 2][:, :], d_gate[kc * 128:(kc + 1) * 128, :],
                          r=GATE_ALL, w=[f"gtr{kc % 2}"])

                load_sloc(0)
                load_yp(0)
                load_gt(0)
                for kc in range(NCH):
                    if kc + 1 < NCH:
                        load_sloc(kc + 1)
                        load_yp(kc + 1)
                        load_gt(kc + 1)
                    if kc > 0:
                        sl = slocb[(kc - 1) % 3]
                        slreg = f"slocb{(kc - 1) % 3}"
                        p.op("g", lambda e, kc=kc: e.tensor_tensor(
                            enterT_state[:, :], enterT_state[:, :],
                            bass.AP(dexp, kc - 1, [[32 * NCH, 128], [NCH, 32], [0, 128]]), ALU.mult),
                            r=["dexp", "enter"], w=["enter"])
                        p.op("g", lambda e, sl=sl: e.tensor_tensor(
                            enterT_state[:, :], enterT_state[:, :], sl[:, :], ALU.add),
                            r=["enter", slreg], w=["enter"])
                    for half in range(2):
                        p.op("v", lambda e, half=half: e.tensor_copy(
                            entb_stage[:, :], enterT_state[:, half * 2048:(half + 1) * 2048]),
                             r=["enter"], w=["entbs"])
                        for g in range(2 * half, 2 * half + 2):
                            p.ops("t", [(lambda e, j=j, g=g, half=half: e.transpose(
                                psT[:, j * 128:(j + 1) * 128],
                                entb_stage[:, (g * 8 + j) * 128 - half * 2048:(g * 8 + j + 1) * 128 - half * 2048],
                                ident_b[:, :]))
                                for j in range(8)], r=["entbs", "ident_b"], w=["psT"])
                            p.op("v", lambda e, g=g: e.tensor_copy(enterT_b[:, g * 1024:(g + 1) * 1024], psT[:, :]),
                                 r=["psT"], w=["enterTb"])
                    ypreg = f"ypr{kc % 2}"
                    gtreg = f"gtr{kc % 2}"
                    for hg in range(8):
                        p.ops("t", [lambda e, hg=hg, kc=kc: e.matmul(
                            psA[:, 0:512], C_sb[:, kc * Q:(kc + 1) * Q],
                            enterT_b[:, hg * 512:(hg + 1) * 512], start=True, stop=True)],
                              r=["C_sb", "enterTb"], w=["psA0"])
                        p.op("v", lambda e, hg=hg, kc=kc: e.tensor_tensor(
                            tmpv2[:, :], psA[:, 0:512],
                            bass.AP(sexpT, kc * HEADS + hg * 8, [[HEADS * NCH, 128], [1, 8], [0, HEADDIM]]),
                            ALU.mult),
                            r=["psA0", f"sexpT{kc}"], w=["tmpv2"])
                        p.op("v", lambda e, hg=hg, kc=kc: e.tensor_tensor(
                            ycha[:, hg * 512:(hg + 1) * 512], tmpv2[:, :],
                            ypr[kc % 2][:, hg * 512:(hg + 1) * 512], ALU.add),
                            r=["tmpv2", ypreg], w=["ycha"])
                    if dbg:
                        p.dma("s", "dbgo", dbg_t["dbg_y"][kc * 128:(kc + 1) * 128, :], ycha[:, :],
                              r=["ycha"], w=["dbgy"])
                    p.op("v", lambda e, kc=kc: e.tensor_tensor(ychb[:, :], ycha[:, :], gtr[kc % 2][:, :], ALU.mult),
                         r=["ycha", gtreg], w=["ychb"])
                    p.op("s", lambda e: e.activation(ycha[:, :], ychb[:, :], AF.Square, accum_out=gss[:, :]),
                         r=["ychb"], w=["ycha", "gss"])
                    p.op("s", lambda e: e.activation(grst[:, :], gss[:, :], AF.Ln, bias=cst[:, 0:1], scale=1.0 / INTER),
                         r=["gss", "cst"], w=["grst"])
                    p.op("s", lambda e: e.activation(grst[:, :], grst[:, :], AF.Exp, scale=-0.5),
                         r=["grst"], w=["grst"])
                    p.op("v", lambda e: e.scalar_tensor_tensor(ynb[:, :], ychb[:, :], grst[:, 0:1], gnw_rep[:, :],
                                                               ALU.mult, ALU.mult),
                         r=["ychb", "grst", "gnw_rep"], w=["ynb"])
                    for g in range(4):
                        p.ops("t", [(lambda e, j=j, g=g: e.transpose(psT[:, j * 128:(j + 1) * 128],
                                                                     ynb[:, (g * 8 + j) * 128:(g * 8 + j + 1) * 128],
                                                                     ident_b[:, :]))
                                    for j in range(8)], r=["ynb", "ident_b"], w=["psT"])
                        p.op("v", lambda e: e.tensor_copy(ynt_sb[:, :], psT[:, :]), r=["psT"], w=["yntsb"])
                        p.dma("s", "yntw", bass.AP(d_ynt, (g * 8) * 128 * T + kc * 128, [[T, 128], [128 * T, 8], [1, 128]]),
                              ynt_sb[:, :], r=["yntsb"], w=[f"dynt{kc}_{g}"])
        YNT_ALL = [f"dynt{kc}_{g}" for kc in range(NCH) for g in range(4)]
        mstack.close()
        open_scopes.remove(mstack)

        p.join()
        _stop('P5a')
        TPG = min(512, T)
        NTG = T // TPG
        NTTG = TPG // 128
        with contextlib.ExitStack() as p5b:
            E5 = p5b.enter_context
            ynT_g = E5(nc.sbuf_tensor("ynT_g", [128, 32 * TPG], BF16))
            wo_sl = [E5(nc.sbuf_tensor(f"wo_sl{i}", [128, 32 * 128], BF16)) for i in range(2)]
            ochf = E5(nc.sbuf_tensor("ochf", [128, 512], F32))
            x2pre = E5(nc.sbuf_tensor("x2pre", [128, NTTG * HID], F32))
            xres = [E5(nc.sbuf_tensor(f"xres{i}", [128, HID], F32)) for i in range(2)]
            mcol = E5(nc.sbuf_tensor("mcol", [128, 1], F32))
            sscol2 = E5(nc.sbuf_tensor("sscol2", [128, 1], F32))
            vcol = E5(nc.sbuf_tensor("vcol", [128, 1], F32))
            rstd2 = E5(nc.sbuf_tensor("rstd2", [128, 1], F32))
            nbcol = E5(nc.sbuf_tensor("nbcol", [128, 1], F32))
            sqscr = E5(nc.sbuf_tensor("sqscr", [128, HID], F32))
            x2til = E5(nc.sbuf_tensor("x2til", [128, HID], F32))
            tq = E5(nc.sbuf_tensor("tq", [128, HID], F32))
            uq = E5(nc.sbuf_tensor("uq", [128, HID], I16))
            c1q = E5(nc.sbuf_tensor("c1q", [128, HID], BF16))
            c2q = E5(nc.sbuf_tensor("c2q", [128, HID], BF16))
            qtmp = E5(nc.sbuf_tensor("qtmp", [128, HID], F32))
            x2q = E5(nc.sbuf_tensor("x2q", [128, HID], BF16))

            def load_wo(et):
                for kg in range(4):
                    p.dma("g", f"wol{et % 2}",
                          bass.AP(wo_sl[et % 2], kg * 8 * 128, [[32 * 128, 128], [128, 8], [1, 128]]),
                          bass.AP(WoT, (kg * 8 * 128) * HID + et * 128, [[HID, 128], [128 * HID, 8], [1, 128]]),
                          w=[f"wosl{et % 2}"])

            for tg in range(NTG):
                p.dma("y", "yntl", bass.AP(ynT_g, 0, [[32 * TPG, 128], [TPG, 32], [1, TPG]]),
                      bass.AP(d_ynt, tg * TPG, [[T, 128], [128 * T, 32], [1, TPG]]),
                      r=YNT_ALL, w=["ynTg"])
                if True:
                    load_wo(0)
                for et in range(NKT):
                    if et + 1 < NKT:
                        load_wo(et + 1)
                    slot = (et % 2) * 1024
                    pk = f"psA{et % 2}"
                    woreg = f"wosl{et % 2}"
                    wos = wo_sl[et % 2]
                    fs = []
                    for k in range(32):
                        fs.append(lambda e, k=k, slot=slot, wos=wos: e.matmul(
                            psA[:, slot:slot + TPG], wos[:, k * 128:(k + 1) * 128],
                            ynT_g[:, k * TPG:(k + 1) * TPG], start=(k == 0), stop=(k == 31)))
                    p.ops("t", fs, r=[woreg, "ynTg"], w=[pk])
                    p.op("v", lambda e, slot=slot: e.tensor_copy(ochf[:, 0:TPG], psA[:, slot:slot + TPG]),
                         r=[pk], w=["ochf"])
                    p.ops("t", [(lambda e, j=j: e.transpose(psB[:, j * 128:(j + 1) * 128],
                                                            ochf[:, j * 128:(j + 1) * 128], ident_f[:, :]))
                                for j in range(NTTG)], r=["ochf", "ident_f"], w=["psB"])
                    p.op("v", lambda e, et=et: e.tensor_copy(
                        bass.AP(x2pre, et * 128, [[NTTG * HID, 128], [HID, NTTG], [1, 128]]),
                        bass.AP(psB, 0, [[1024, 128], [128, NTTG], [1, 128]])),
                        r=["psB"], w=["x2pre"])
                for tt in range(NTTG):
                    ti = tg * NTTG + tt
                    bi = ti % 2
                    p.dma("y", f"xrl{bi}", xres[bi][:, :], x_sh[3 + ti * 128:3 + (ti + 1) * 128, :], w=[f"xres{bi}"])
                    blk = x2pre[:, tt * HID:(tt + 1) * HID]
                    p.op("v", lambda e, blk=blk, bi=bi: e.scalar_tensor_tensor(
                        blk, xres[bi][:, :], 2.0, blk, ALU.mult, ALU.add),
                         r=["x2pre", f"xres{bi}"], w=["x2pre"])
                    p.op("v", lambda e, blk=blk: e.reduce_sum(mcol[:, :], blk, axis=mybir.AxisListType.X),
                         r=["x2pre"], w=["mcol"])
                    p.op("v", lambda e: e.tensor_scalar_mul(mcol[:, :], mcol[:, :], 1.0 / HID),
                         r=["mcol"], w=["mcol"])
                    p.op("s", lambda e, blk=blk: e.activation(sqscr[:, :], blk, AF.Square, accum_out=sscol2[:, :]),
                         r=["x2pre"], w=["sqscr", "sscol2"])
                    p.op("v", lambda e: e.scalar_tensor_tensor(vcol[:, :], mcol[:, 0:1], 0.0, mcol[:, 0:1],
                                                               ALU.add, ALU.mult),
                         r=["mcol"], w=["vcol"])
                    p.op("v", lambda e: e.scalar_tensor_tensor(vcol[:, :], sscol2[:, 0:1], 1.0 / HID, vcol[:, 0:1],
                                                               ALU.mult, ALU.subtract),
                         r=["sscol2", "vcol"], w=["vcol"])
                    p.op("s", lambda e: e.activation(rstd2[:, :], vcol[:, :], AF.Ln, bias=cst[:, 0:1]),
                         r=["vcol", "cst"], w=["rstd2"])
                    p.op("s", lambda e: e.activation(rstd2[:, :], rstd2[:, :], AF.Exp, scale=-0.5),
                         r=["rstd2"], w=["rstd2"])
                    p.op("v", lambda e: e.tensor_tensor(nbcol[:, :], mcol[:, :], rstd2[:, :], ALU.mult),
                         r=["mcol", "rstd2"], w=["nbcol"])
                    p.op("v", lambda e: e.tensor_scalar_mul(nbcol[:, :], nbcol[:, :], -1.0),
                         r=["nbcol"], w=["nbcol"])
                    p.op("s", lambda e, blk=blk: e.activation(x2til[:, :], blk, AF.Identity, bias=nbcol[:, 0:1],
                                                              scale=rstd2[:, 0:1]),
                         r=["x2pre", "nbcol", "rstd2"], w=["x2til"])
                    p.dma("s", "x2sp", d_x2[ti * 128:(ti + 1) * 128, :], x2til[:, :], r=["x2til"], w=[f"dx2{ti}"])
                    p.op("v", lambda e: e.reduce_max(mcol[:, :], x2til[:, :], axis=mybir.AxisListType.X,
                                                     apply_absolute_value=True),
                         r=["x2til"], w=["mcol"])
                    p.op("v", lambda e: e.tensor_scalar_max(mcol[:, :], mcol[:, :], 1e-5), r=["mcol"], w=["mcol"])
                    p.op("v", lambda e, ti=ti: e.tensor_scalar_mul(x2inv[:, ti:ti + 1], mcol[:, :], 1.0 / 127.0),
                         r=["mcol"], w=["x2inv"])
                    p.op("v", lambda e: e.reciprocal(vcol[:, :], mcol[:, :]), r=["mcol"], w=["vcol"])
                    p.op("v", lambda e, ti=ti: e.tensor_scalar_mul(x2scl[:, ti:ti + 1], vcol[:, :], 127.0),
                         r=["vcol"], w=["x2scl"])
                    p.op("s", lambda e, ti=ti: e.activation(tq[:, :], x2til[:, :], AF.Identity,
                                                            scale=x2scl[:, ti:ti + 1]),
                         r=["x2til", "x2scl"], w=["tq"])
                    p.op("s", lambda e: e.copy(uq[:, :], tq[:, :]), r=["tq"], w=["uq"])
                    p.op("v", lambda e: e.scalar_tensor_tensor(c1q[:, :], uq[:, :], 0.5, tq[:, :],
                                                               ALU.add, ALU.is_le),
                         r=["uq", "tq"], w=["c1q"])
                    p.op("v", lambda e: e.scalar_tensor_tensor(c2q[:, :], uq[:, :], -0.5, tq[:, :],
                                                               ALU.add, ALU.is_ge),
                         r=["uq", "tq"], w=["c2q"])
                    p.op("v", lambda e: e.tensor_tensor(qtmp[:, :], uq[:, :], c1q[:, :], ALU.add),
                         r=["uq", "c1q"], w=["qtmp"])
                    p.op("v", lambda e: e.tensor_tensor(x2q[:, :], qtmp[:, :], c2q[:, :], ALU.subtract),
                         r=["qtmp", "c2q"], w=["x2q"])
                    for h in range(2):
                        p.ops("t", [(lambda e, j=j, h=h: e.transpose(psT[:, j * 128:(j + 1) * 128],
                                                                     x2q[:, (h * 8 + j) * 128:(h * 8 + j + 1) * 128],
                                                                     ident_b[:, :]))
                                    for j in range(8)], r=["x2q", "ident_b"], w=["psT"])
                        p.op("v", lambda e: e.tensor_copy(c1q[:, 0:1024], psT[:, :]), r=["psT"], w=["c1q"])
                        p.dma("s", "x2iw", bass.AP(d_x2i, (h * 8) * 128 * T + ti * 128,
                                                   [[T, 128], [128 * T, 8], [1, 128]]),
                              c1q[:, 0:1024], r=["c1q"], w=[f"dx2i{ti}_{h}"])
        X2I_ALL = [f"dx2i{ti}_{h}" for ti in range(NTT_R) for h in range(2)]
        if dbg:
            p.join()
            for ti in range(NTT_R):
                p.dma("g", "dbgo2", dbg_t["dbg_x2"][ti * 128:(ti + 1) * 128, :], d_x2[ti * 128:(ti + 1) * 128, :],
                      r=[f"dx2{ti}"], w=["dbgx2"])

        p.join()
        _stop('P5b')
        p.dma("g", "sct", d_tmp[0:1, 0:FFH], bass.AP(ag_sc_out, 0, [[W1SL + W2SL, NCORES], [1, W1SL]]),
              r=["AGSC"], w=["dtmpw1"])
        with contextlib.ExitStack() as p6:
            E6 = p6.enter_context
            x2iT = E6(nc.sbuf_tensor("x2iT", [128, NKT * T], BF16))
            w1sl = E6(nc.sbuf_tensor("w1sl", [128, NKT * 1024], BF16))
            wsrow = E6(nc.sbuf_tensor("wsrow", [1, 1024], F32))
            ws_sl = E6(nc.sbuf_tensor("ws_sl", [128, 1024], F32))
            h1 = E6(nc.sbuf_tensor("h1", [128, 1024], F32))
            h2 = E6(nc.sbuf_tensor("h2", [128, 1024], F32))
            sq6 = E6(nc.sbuf_tensor("sq6", [128, 1024], F32))
            m6 = E6(nc.sbuf_tensor("m6", [128, 1024], F32))
            u6 = E6(nc.sbuf_tensor("u6", [128, 1024], F32))
            g6 = E6(nc.sbuf_tensor("g6", [128, 1024], F32))
            hgel = E6(nc.sbuf_tensor("hgel", [128, 1024], F32))
            hmc = E6(nc.sbuf_tensor("hmc", [128, 1], F32))
            p.dma("y", "x2il", bass.AP(x2iT, 0, [[NKT * T, 128], [T, NKT], [1, T]]),
                  bass.AP(d_x2i, 0, [[T, 128], [128 * T, NKT], [1, T]]), r=X2I_ALL, w=["x2iT"])
            for hp in range(FFH // 1024):
                p.dma("y", "w1l", bass.AP(w1sl, 0, [[NKT * 1024, 128], [1024, NKT], [1, 1024]]),
                      bass.AP(ag_w1_out, hp * HID * W1SL, [[W1SL, 128], [128 * W1SL, NKT], [1, 1024]]),
                      r=["AGW1"], w=["w1sl"])
                p.dma("y", "wsr", wsrow[0:1, :], d_tmp[0:1, hp * 1024:(hp + 1) * 1024], r=["dtmpw1"], w=["wsrow"])
                p.ops("t", [(lambda e, i=i: e.matmul(psB[:, i * 512:(i + 1) * 512], ones1f[:, :],
                                                     wsrow[0:1, i * 512:(i + 1) * 512], start=True, stop=True))
                            for i in range(2)], r=["ones1f", "wsrow"], w=["psB"])
                p.op("v", lambda e: e.tensor_copy(ws_sl[:, :], psB[:, :]), r=["psB"], w=["ws_sl"])
                for tt in range(NTT_R):
                    slot = (tt % 2) * 1024
                    pk = f"psA{tt % 2}"
                    fs = []
                    for k in range(NKT):
                        for half2 in range(2):
                            fs.append(lambda e, k=k, half2=half2, tt=tt, slot=slot: e.matmul(
                                psA[:, slot + half2 * 512: slot + (half2 + 1) * 512],
                                x2iT[:, k * T + tt * 128:k * T + (tt + 1) * 128],
                                w1sl[:, k * 1024 + half2 * 512:k * 1024 + (half2 + 1) * 512],
                                start=(k == 0), stop=(k == NKT - 1)))
                    p.ops("t", fs, r=["w1sl", "x2iT"], w=[pk])
                    p.op("s", lambda e, slot=slot, tt=tt: e.activation(h1[:, :], psA[:, slot:slot + 1024],
                                                                       AF.Identity, scale=x2inv[:, tt:tt + 1]),
                         r=[pk, "x2inv"], w=["h1"])
                    p.op("g", lambda e: e.tensor_tensor(h2[:, :], h1[:, :], ws_sl[:, :], ALU.mult),
                         r=["h1", "ws_sl"], w=["h2"])
                    p.op("s", lambda e: e.activation(sq6[:, :], h2[:, :], AF.Square), r=["h2"], w=["sq6"])
                    p.op("v", lambda e: e.tensor_scalar(m6[:, :], sq6[:, :], GELU_B, GELU_A, ALU.mult, ALU.add),
                         r=["sq6"], w=["m6"])
                    p.op("g", lambda e: e.tensor_tensor(u6[:, :], m6[:, :], h2[:, :], ALU.mult),
                         r=["m6", "h2"], w=["u6"])
                    p.op("s", lambda e: e.activation(g6[:, :], u6[:, :], AF.Sigmoid), r=["u6"], w=["g6"])
                    p.op("v", lambda e: e.tensor_tensor(hgel[:, :], g6[:, :], h2[:, :], ALU.mult),
                         r=["g6", "h2"], w=["hgel"])
                    p.op("v", lambda e: e.reduce_max(hmc[:, :], hgel[:, :], axis=mybir.AxisListType.X,
                                                     apply_absolute_value=True),
                         r=["hgel"], w=["hmc"])
                    if hp == 0:
                        p.op("v", lambda e, tt=tt: e.tensor_copy(hmax_run[:, tt:tt + 1], hmc[:, :]),
                             r=["hmc"], w=["hmax"])
                    else:
                        p.op("v", lambda e, tt=tt: e.tensor_tensor(hmax_run[:, tt:tt + 1], hmax_run[:, tt:tt + 1],
                                                                   hmc[:, :], ALU.max),
                             r=["hmc", "hmax"], w=["hmax"])
                    p.dma("s", "hgsp", d_hg[tt * 128:(tt + 1) * 128, hp * 1024:(hp + 1) * 1024], hgel[:, :],
                          r=["hgel"], w=[f"dhg{hp}_{tt}"])
        HG_ALL = [f"dhg{hp}_{tt}" for hp in range(FFH // 1024) for tt in range(NTT_R)]
        if dbg:
            p.join()
            for tt in range(NTT_R):
                p.dma("g", "dbgo2", dbg_t["dbg_hg"][tt * 128:(tt + 1) * 128, :], d_hg[tt * 128:(tt + 1) * 128, :],
                      r=HG_ALL, w=["dbghg"])
            p.dma("g", "dbgo2", dbg_t["dbg_w1q"][:, :], ag_w1_out[:, :], r=["AGW1"], w=["dbgw1"])
            p.dma("g", "dbgo2", dbg_t["dbg_w2q"][:, :], ag_w2_out[:, :], r=["AGW2"], w=["dbgw2"])
            p.dma("g", "dbgo2", dbg_t["dbg_sc"][:, :], ag_sc_out[:, :], r=["AGSC"], w=["dbgsc"])

        p.join()
        _stop('P6')
        for tt in range(NTT_R):
            p.op("v", lambda e, tt=tt: e.tensor_scalar_max(hmax_run[:, tt:tt + 1], hmax_run[:, tt:tt + 1], 1e-5),
                 r=["hmax"], w=["hmax"])
        p.op("v", lambda e: e.tensor_scalar_mul(hinv[:, :], hmax_run[:, :], 1.0 / 127.0), r=["hmax"], w=["hinv"])
        p.op("v", lambda e: e.reciprocal(hscl[:, :], hmax_run[:, :]), r=["hmax"], w=["hscl"])
        p.op("v", lambda e: e.tensor_scalar_mul(hscl[:, :], hscl[:, :], 127.0), r=["hscl"], w=["hscl"])
        hvt = E(nc.sbuf_tensor("hvt", [NTT_R, 128], F32))
        p.ops("t", [lambda e: e.transpose(psB[0:NTT_R, 0:128], hinv[:, 0:NTT_R], ident_f[:, :])],
              r=["hinv", "ident_f"], w=["psB"])
        p.op("v", lambda e: e.tensor_copy(hvt[:, :], psB[0:NTT_R, 0:128]),
             r=["psB"], w=["hinvT"])
        p.dma("g", "sct", d_tmp[0:1, 0:T], bass.AP(hvt, 0, [[128, NTT_R], [1, 128]]), r=["hinvT"], w=["dtmph"])
        with contextlib.ExitStack() as p7:
            E7 = p7.enter_context
            hgr = [E7(nc.sbuf_tensor(f"hgr{i}", [128, 1024], F32)) for i in range(2)]
            tq7 = [E7(nc.sbuf_tensor(f"tq7{i}", [128, 1024], F32)) for i in range(2)]
            uq7 = [E7(nc.sbuf_tensor(f"uq7{i}", [128, 1024], I16)) for i in range(2)]
            c1q7 = [E7(nc.sbuf_tensor(f"c1q7{i}", [128, 1024], BF16)) for i in range(2)]
            c2q7 = [E7(nc.sbuf_tensor(f"c2q7{i}", [128, 1024], BF16)) for i in range(2)]
            qt7 = [E7(nc.sbuf_tensor(f"qt7{i}", [128, 1024], F32)) for i in range(2)]
            hq7 = [E7(nc.sbuf_tensor(f"hq7{i}", [128, 1024], BF16)) for i in range(2)]
            hts = [E7(nc.sbuf_tensor(f"hts{i}", [128, 1024], BF16)) for i in range(2)]
            idx = 0
            for hp in range(FFH // 1024):
                for tt in range(NTT_R):
                    bi = idx % 2
                    idx += 1
                    tq_, uq_, c1_, c2_, qt_, hq_, ht_ = (tq7[bi], uq7[bi], c1q7[bi], c2q7[bi],
                                                         qt7[bi], hq7[bi], hts[bi])
                    p.dma("y", f"hgl{bi}", hgr[bi][:, :], d_hg[tt * 128:(tt + 1) * 128, hp * 1024:(hp + 1) * 1024],
                          r=[f"dhg{hp}_{tt}"], w=[f"hgr{bi}"])
                    p.op("s", lambda e, bi=bi, tt=tt, tq_=tq_: e.activation(tq_[:, :], hgr[bi][:, :], AF.Identity,
                                                                            scale=hscl[:, tt:tt + 1]),
                         r=[f"hgr{bi}", "hscl"], w=[f"tq7{bi}"])
                    p.op("s", lambda e, tq_=tq_, uq_=uq_: e.copy(uq_[:, :], tq_[:, :]),
                         r=[f"tq7{bi}"], w=[f"uq7{bi}"])
                    p.op("v", lambda e, uq_=uq_, tq_=tq_, c1_=c1_: e.scalar_tensor_tensor(
                        c1_[:, :], uq_[:, :], 0.5, tq_[:, :], ALU.add, ALU.is_le),
                         r=[f"uq7{bi}", f"tq7{bi}"], w=[f"c1q7{bi}"])
                    p.op("v", lambda e, uq_=uq_, tq_=tq_, c2_=c2_: e.scalar_tensor_tensor(
                        c2_[:, :], uq_[:, :], -0.5, tq_[:, :], ALU.add, ALU.is_ge),
                         r=[f"uq7{bi}", f"tq7{bi}"], w=[f"c2q7{bi}"])
                    p.op("v", lambda e, uq_=uq_, c1_=c1_, qt_=qt_: e.tensor_tensor(
                        qt_[:, :], uq_[:, :], c1_[:, :], ALU.add),
                         r=[f"uq7{bi}", f"c1q7{bi}"], w=[f"qt7{bi}"])
                    p.op("v", lambda e, qt_=qt_, c2_=c2_, hq_=hq_: e.tensor_tensor(
                        hq_[:, :], qt_[:, :], c2_[:, :], ALU.subtract),
                         r=[f"qt7{bi}", f"c2q7{bi}"], w=[f"hq7{bi}"])
                    psT_ = psT if bi == 0 else psT2
                    psTreg = "psT" if bi == 0 else "psT2"
                    p.ops("t", [(lambda e, j=j, hq_=hq_, psT_=psT_: e.transpose(
                        psT_[:, j * 128:(j + 1) * 128],
                        hq_[:, j * 128:(j + 1) * 128], ident_b[:, :]))
                                for j in range(8)], r=[f"hq7{bi}", "ident_b"], w=[psTreg])
                    p.op("v", lambda e, ht_=ht_, psT_=psT_: e.tensor_copy(ht_[:, :], psT_[:, :]),
                         r=[psTreg], w=[f"hts{bi}"])
                    p.dma("s", "hiw", bass.AP(d_hint, (hp * 8) * 128 * T + tt * 128, [[T, 128], [128 * T, 8], [1, 128]]),
                          ht_[:, :], r=[f"hts{bi}"], w=[f"dhi{hp}_{tt}"])
        HI_ALL = [f"dhi{hp}_{tt}" for hp in range(FFH // 1024) for tt in range(NTT_R)]
        if dbg:
            p.join()
            for hp in range(FFH // 1024):
                p.dma("g", "dbgo2", dbg_t["dbg_hint"][hp * 1024:(hp + 1) * 1024, :],
                      d_hint[hp * 1024:(hp + 1) * 1024, :], r=HI_ALL, w=["dbghi"])

        p.join()
        _stop('P7')
        p.dma("g", "sct", d_tmp[0:1, 4096:4096 + HID],
              bass.AP(ag_sc_out, W1SL, [[W1SL + W2SL, NCORES], [1, W2SL]]), r=["AGSC"], w=["dtmpw2"])
        p.dma("y", "ws2l", ws2col[:, :], bass.AP(d_tmp, 4096, [[1, 128], [128, NKT]]), r=["dtmpw2"], w=["ws2c"], slow=True)
        with contextlib.ExitStack() as p8:
            E8 = p8.enter_context
            hslab = E8(nc.sbuf_tensor("hslab", [128, 64 * TPG], BF16))
            mcol2 = E8(nc.sbuf_tensor("mcol2", [128, 1], F32))
            ss8 = E8(nc.sbuf_tensor("ss8", [128, 1], F32))
            vc8 = E8(nc.sbuf_tensor("vc8", [128, 1], F32))
            rs8 = E8(nc.sbuf_tensor("rs8", [128, 1], F32))
            nb8 = E8(nc.sbuf_tensor("nb8", [128, 1], F32))
            w2sl = [E8(nc.sbuf_tensor(f"w2sl{i}", [128, 64 * 128], BF16)) for i in range(2)]
            s2xr = E8(nc.sbuf_tensor("s2xr", [128, T], F32))
            o2f = E8(nc.sbuf_tensor("o2f", [128, 512], F32))
            x3g = E8(nc.sbuf_tensor("x3g", [128, NTTG * HID], F32))
            x2r = [E8(nc.sbuf_tensor(f"x2r{i}", [128, HID], F32)) for i in range(2)]
            sq8 = E8(nc.sbuf_tensor("sq8", [128, HID], F32))
            fin = E8(nc.sbuf_tensor("fin", [128, HID], F32))
            grow8 = E8(nc.sbuf_tensor("grow8", [1, T], F32))
            p.dma("y", "wsr2", grow8[0:1, :], d_tmp[0:1, 0:T], r=["dtmph"], w=["grow8"])
            for (c0, w) in colsplits(T, 2048):
                fs = []
                for (s0, sw) in colsplits(w, 512):
                    fs.append(lambda e, c0=c0, s0=s0, sw=sw: e.matmul(psA[:, s0:s0 + sw], ones1f[:, :],
                                                                     grow8[0:1, c0 + s0:c0 + s0 + sw],
                                                                     start=True, stop=True))
                p.ops("t", fs, r=["ones1f", "grow8"], w=PSA01)
                p.op("v", lambda e, c0=c0, w=w: e.tensor_copy(s2xr[:, c0:c0 + w], psA[:, 0:w]),
                     r=PSA01, w=["s2xr"])

            def load_w2(et):
                rb = et // 2
                half = et % 2
                for kg in range(4):
                    p.dma("g", f"w2l{et % 2}",
                          bass.AP(w2sl[et % 2], kg * 16 * 128, [[64 * 128, 128], [128, 16], [1, 128]]),
                          bass.AP(ag_w2_out, (rb * FFH + kg * 16 * 128) * W2SL + half * 128,
                                  [[W2SL, 128], [128 * W2SL, 16], [1, 128]]),
                          w=[f"w2sl{et % 2}"])

            for tc in range(NTG):
                p.dma("y", "hil", bass.AP(hslab, 0, [[64 * TPG, 128], [TPG, 64], [1, TPG]]),
                      bass.AP(d_hint, tc * TPG, [[T, 128], [128 * T, 64], [1, TPG]]),
                      r=HI_ALL, w=["hslab"])
                if True:
                    load_w2(0)
                for et in range(NKT):
                    if et + 1 < NKT:
                        load_w2(et + 1)
                    slot = (et % 2) * 1024
                    pk = f"psA{et % 2}"
                    w2reg = f"w2sl{et % 2}"
                    w2s = w2sl[et % 2]
                    fs = []
                    for k in range(64):
                        fs.append(lambda e, k=k, slot=slot, w2s=w2s: e.matmul(
                            psA[:, slot:slot + TPG], w2s[:, k * 128:(k + 1) * 128],
                            hslab[:, k * TPG:(k + 1) * TPG], start=(k == 0), stop=(k == 63)))
                    p.ops("t", fs, r=[w2reg, "hslab"], w=[pk])
                    p.op("s", lambda e, slot=slot, et=et: e.activation(o2f[:, 0:TPG], psA[:, slot:slot + TPG],
                                                                       AF.Identity, scale=ws2col[:, et:et + 1]),
                         r=[pk, "ws2c"], w=["o2f"])
                    p.op("g", lambda e, tc=tc: e.tensor_tensor(o2f[:, 0:TPG], o2f[:, 0:TPG],
                                                               s2xr[:, tc * TPG:(tc + 1) * TPG], ALU.mult),
                         r=["o2f", "s2xr"], w=["o2f"])
                    p.ops("t", [(lambda e, j=j: e.transpose(psB[:, j * 128:(j + 1) * 128],
                                                            o2f[:, j * 128:(j + 1) * 128], ident_f[:, :]))
                                for j in range(NTTG)], r=["o2f", "ident_f"], w=["psB"])
                    p.op("v", lambda e, et=et: e.tensor_copy(
                        bass.AP(x3g, et * 128, [[NTTG * HID, 128], [HID, NTTG], [1, 128]]),
                        bass.AP(psB, 0, [[1024, 128], [128, NTTG], [1, 128]])),
                        r=["psB"], w=["x3g"])
                for tt in range(NTTG):
                    ti = tc * NTTG + tt
                    bi = ti % 2
                    p.dma("y", f"x2rl{bi}", x2r[bi][:, :], d_x2[ti * 128:(ti + 1) * 128, :],
                          r=[f"dx2{ti}"], w=[f"x2r{bi}"])
                    blk = x3g[:, tt * HID:(tt + 1) * HID]
                    p.op("v", lambda e, blk=blk, bi=bi: e.tensor_tensor(blk, blk, x2r[bi][:, :], ALU.add),
                         r=["x3g", f"x2r{bi}"], w=["x3g"])
                    if dbg:
                        p.dma("s", "dbgo", dbg_t["dbg_ff"][ti * 128:(ti + 1) * 128, :], blk,
                              r=["x3g"], w=["dbgff"])
                    p.op("v", lambda e, blk=blk: e.reduce_sum(mcol2[:, :], blk, axis=mybir.AxisListType.X),
                         r=["x3g"], w=["mcol2"])
                    p.op("v", lambda e: e.tensor_scalar_mul(mcol2[:, :], mcol2[:, :], 1.0 / HID),
                         r=["mcol2"], w=["mcol2"])
                    p.op("s", lambda e, blk=blk: e.activation(sq8[:, :], blk, AF.Square, accum_out=ss8[:, :]),
                         r=["x3g"], w=["sq8", "ss8"])
                    p.op("v", lambda e: e.scalar_tensor_tensor(vc8[:, :], mcol2[:, 0:1], 0.0, mcol2[:, 0:1],
                                                               ALU.add, ALU.mult), r=["mcol2"], w=["vc8"])
                    p.op("v", lambda e: e.scalar_tensor_tensor(vc8[:, :], ss8[:, 0:1], 1.0 / HID, vc8[:, 0:1],
                                                               ALU.mult, ALU.subtract), r=["ss8", "vc8"], w=["vc8"])
                    p.op("s", lambda e: e.activation(rs8[:, :], vc8[:, :], AF.Ln, bias=cst[:, 0:1]),
                         r=["vc8", "cst"], w=["rs8"])
                    p.op("s", lambda e: e.activation(rs8[:, :], rs8[:, :], AF.Exp, scale=-0.5), r=["rs8"], w=["rs8"])
                    p.op("v", lambda e: e.tensor_tensor(nb8[:, :], mcol2[:, :], rs8[:, :], ALU.mult),
                         r=["mcol2", "rs8"], w=["nb8"])
                    p.op("v", lambda e: e.tensor_scalar_mul(nb8[:, :], nb8[:, :], -1.0), r=["nb8"], w=["nb8"])
                    p.op("s", lambda e, blk=blk: e.activation(fin[:, :], blk, AF.Identity, bias=nb8[:, 0:1],
                                                              scale=rs8[:, 0:1]),
                         r=["x3g", "nb8", "rs8"], w=["fin"])
                    p.dma("s", "outw", out_y[ti * 128:(ti + 1) * 128, :], fin[:, :], r=["fin"], w=[f"outy{ti}"])
        p.join(skip_cc=False)
        p.emit_all()
        ctx.close()
    except _StopBuild:
        pass
    return nc, p, dbg_t


def _host_prep(inputs, T=2048):
    x = np.ascontiguousarray(inputs["L0_attn_hidden_states"], dtype=np.float32)
    Bb, S, _ = x.shape
    segs = NCORES // Bb
    WpT = np.ascontiguousarray(np.asarray(inputs["L0_attn_in_proj_w"], dtype=np.float32).T)
    WoT = np.ascontiguousarray(np.asarray(inputs["L0_attn_out_proj_w"], dtype=np.float32).T)
    w1T = np.ascontiguousarray(np.asarray(inputs["L0_ffn_w1"], dtype=np.float32).T)
    w2T = np.ascontiguousarray(np.asarray(inputs["L0_ffn_w2"], dtype=np.float32).T)
    conv_w = np.asarray(inputs["L0_attn_conv_w"], dtype=np.float32).reshape(CONVD, KCONV)
    convdiag = np.zeros((N_XBC_T, KCONV, 128, 128), dtype=np.float32)
    for ct in range(N_XBC_T):
        for tap in range(KCONV):
            np.fill_diagonal(convdiag[ct, tap], conv_w[ct * 128:(ct + 1) * 128, tap])
    convdiag = convdiag.reshape(N_XBC_T * KCONV * 128, 128)
    ident = np.eye(128, dtype=np.float32)
    tri01 = np.triu(np.ones((128, 128), dtype=np.float32))
    pexp = np.zeros((HEADS, INTER), dtype=np.float32)
    for hp in range(INTER):
        pexp[hp // HEADDIM, hp] = 1.0
    shared = dict(
        WpT=WpT, WoT=WoT, convdiag=convdiag,
        conv_b=np.ascontiguousarray(inputs["L0_attn_conv_b"], dtype=np.float32),
        nw_in=np.ascontiguousarray(inputs["L0_attn_norm_weight"], dtype=np.float32),
        gnw_in=np.ascontiguousarray(inputs["L0_attn_gate_norm_weight"], dtype=np.float32),
        dtb_in=np.ascontiguousarray(inputs["L0_attn_dt_bias"], dtype=np.float32),
        alog_in=np.ascontiguousarray(inputs["L0_attn_A_log"], dtype=np.float32),
        dd_in=np.ascontiguousarray(inputs["L0_attn_D"], dtype=np.float32),
        ident_in=ident, tri01_in=tri01, pexp_in=pexp,
    )
    in_maps = []
    for c in range(NCORES):
        b, q = c // segs, c % segs
        t0 = q * T
        xs = np.empty((T + 3, HID), dtype=np.float32)
        xs[3:] = x[b, t0:t0 + T]
        if q == 0:
            xs[:3] = 0.0
        else:
            xs[:3] = x[b, t0 - 3:t0]
        m2l = np.zeros((9, NCORES), dtype=np.float32)
        for r in range(NCORES):
            for t in range(NCORES):
                if r // segs == b and t // segs == b and (r % segs) < (t % segs) < q:
                    m2l[t, r] = 1.0
        for r in range(NCORES):
            ok = (r // segs == b) and (r % segs) < q
            m2l[8, r] = 0.0 if ok else -1e30
        m = dict(shared)
        m["x_sh"] = xs
        m["m2l_in"] = m2l
        m["w1T_sl"] = np.ascontiguousarray(w1T[:, c * (FFH // NCORES):(c + 1) * (FFH // NCORES)])
        m["w2T_sl"] = np.ascontiguousarray(w2T[:, c * (HID // NCORES):(c + 1) * (HID // NCORES)])
        in_maps.append(m)
    return in_maps


_CACHE = {}


def kernel(**inputs):
    T = 2048
    if "nc" not in _CACHE:
        nc, p, dbg_t = build(T, dbg=False)
        _CACHE["nc"] = nc
    in_maps = _host_prep(inputs, T)
    res = run_bass_kernel_spmd(_CACHE["nc"], in_maps, core_ids=list(range(NCORES)))
    x = inputs["L0_attn_hidden_states"]
    Bb, S, _ = np.asarray(x).shape
    segs = NCORES // Bb
    out = np.empty((Bb, S, HID), dtype=np.float32)
    for c in range(NCORES):
        out[c // segs, (c % segs) * T:(c % segs + 1) * T] = res.results[c]["out_y"]
    return out
